# version 28
# baseline (speedup 1.0000x reference)
import numpy as np
import ml_dtypes
import concourse.bass as bass
import concourse.mybir as mybir
from concourse.bass_utils import run_bass_kernel_spmd

F32 = mybir.dt.float32
BF16 = mybir.dt.bfloat16
_EAGER_WPREP = False
FP8 = mybir.dt.float8e4
I32 = mybir.dt.int32
ALU = mybir.AluOpType
ACTF = mybir.ActivationFunctionType
AX = mybir.AxisListType
AP = bass.AP

T = 2048
NB = 8
BT = 256
D = 1024
DC = 8
HN = 4096
HC = 32
VW = 66
SCALE = 0.125
H1W = 259
X2W = 257
YW = 130


_TAPS = ()


class Sem:
    __slots__ = ("h", "n")

    def __init__(self, h):
        self.h = h
        self.n = 0


def _win(h):
    if h == 0:
        return 0, 255
    if h == NB - 1:
        return 256 * h - 1, 257
    return 256 * h - 1, 256


def build_program(taps=()):
    global _TAPS
    _TAPS = tuple(taps)
    nc = bass.Bass("TRN2", target_bir_lowering=False,
                   dynamic_dma_scratch_size=4096)

    xT = nc.dram_tensor("xT", [D, T], F32, kind="ExternalInput")
    xTh = nc.dram_tensor("xTh", [D, BT], F32, kind="ExternalInput")
    flags_t = nc.dram_tensor("flags", [128, 2], F32, kind="ExternalInput")
    wqT = nc.dram_tensor("wqT", [D, D], F32, kind="ExternalInput")
    wkT = nc.dram_tensor("wkT", [D, D], F32, kind="ExternalInput")
    wvT = nc.dram_tensor("wvT", [D, D], F32, kind="ExternalInput")
    woT = nc.dram_tensor("woT", [D, D], F32, kind="ExternalInput")
    w1Tt = nc.dram_tensor("w1T", [D, HN], F32, kind="ExternalInput")
    w2Tt = nc.dram_tensor("w2T", [HN, D], F32, kind="ExternalInput")
    consts_t = nc.dram_tensor("consts", [128, 392], F32, kind="ExternalInput")
    cwb_t = nc.dram_tensor("cwb", [128, 128], F32, kind="ExternalInput")
    outT = nc.dram_tensor("outT", [D, T], F32, kind="ExternalOutput")
    tails_t = nc.dram_tensor("out_tails", [128, DC * NB], F32,
                             kind="ExternalOutput")
    dbgf_t = nc.dram_tensor("dbgf", [128, 32768], F32, kind="ExternalOutput")
    dbgb_t = nc.dram_tensor("dbgb", [128, 32768], BF16, kind="ExternalOutput")

    Q = {k: [] for k in ("sync", "tensor", "vector", "scalar", "gpsimd")}

    from contextlib import ExitStack
    _es = ExitStack()
    x0 = _es.enter_context(nc.sbuf_tensor("x0", [128, DC * BT], F32))
    x1 = _es.enter_context(nc.sbuf_tensor("x1", [128, DC * BT], F32))
    xn = _es.enter_context(nc.sbuf_tensor("xn", [128, DC * BT], BF16))
    sqb = _es.enter_context(nc.sbuf_tensor("sqb", [128, DC * 260], BF16))
    rowbank = _es.enter_context(nc.sbuf_tensor("rowbank", [128, 1300], F32))
    reps = _es.enter_context(nc.sbuf_tensor("reps", [128, 896], F32))
    xq = _es.enter_context(nc.sbuf_tensor("xq", [128, DC * BT], BF16))
    xk = _es.enter_context(nc.sbuf_tensor("xk", [128, DC * BT], BF16))
    qrhs = _es.enter_context(nc.sbuf_tensor("qrhs", [128, 2 * 3 * BT], BF16))
    v65 = _es.enter_context(nc.sbuf_tensor("v65", [128, 2 * 16 * VW], BF16))
    ptb = _es.enter_context(nc.sbuf_tensor("ptb", [128, 2064], BF16))
    stage = _es.enter_context(nc.sbuf_tensor("stage", [128, 256], BF16))
    rcol = _es.enter_context(nc.sbuf_tensor("rcol", [128, 8], F32))
    smallf = _es.enter_context(nc.sbuf_tensor("smallf", [128, 16], F32))
    wqb = _es.enter_context(nc.sbuf_tensor("wqb", [128, DC * D], BF16))
    wkb = _es.enter_context(nc.sbuf_tensor("wkb", [128, DC * D], BF16))
    wvb = _es.enter_context(nc.sbuf_tensor("wvb", [128, DC * D], BF16))
    wos0 = _es.enter_context(nc.sbuf_tensor("wos0", [128, D], BF16))
    wos1 = _es.enter_context(nc.sbuf_tensor("wos1", [128, D], BF16))
    w1q = _es.enter_context(nc.sbuf_tensor("w1q", [128, DC * HN], FP8))
    w2q = _es.enter_context(nc.sbuf_tensor("w2q", [128, HC * D], FP8))
    constf = _es.enter_context(nc.sbuf_tensor("constf", [128, 392], F32))
    cwbf = _es.enter_context(nc.sbuf_tensor("cwbf", [128, 128], F32))
    flagf = _es.enter_context(nc.sbuf_tensor("flagf", [128, 2], F32))
    identb = _es.enter_context(nc.sbuf_tensor("identb", [128, 128], BF16))
    swapb = _es.enter_context(nc.sbuf_tensor("swapb", [128, 128], BF16))
    onesb = _es.enter_context(nc.sbuf_tensor("onesb", [128, 4], BF16))
    h1L = _es.enter_context(nc.sbuf_tensor("h1L", [128, HC], BF16))
    h1R = _es.enter_context(nc.sbuf_tensor("h1R", [128, HC], BF16))
    qc = _es.enter_context(nc.sbuf_tensor("qc", [128, 16], BF16))
    hcol = _es.enter_context(nc.sbuf_tensor("hcol", [128, 32], BF16))
    in2 = _es.enter_context(nc.sbuf_tensor("in2", [8, 128], BF16))
    onorm = _es.enter_context(nc.sbuf_tensor("onorm", [16, 64], BF16))
    tails_s = _es.enter_context(nc.sbuf_tensor("tails", [128, DC * NB], F32))
    carry_xm = _es.enter_context(nc.sbuf_tensor("carry_xm", [128, DC], F32))
    carry_h1 = _es.enter_context(nc.sbuf_tensor("carry_h1", [128, 2 * HC], BF16))
    h1b = _es.enter_context(nc.sbuf_tensor("h1b", [128, HC * H1W], BF16))
    yb = _es.enter_context(nc.sbuf_tensor("yb", [128, HC * YW], BF16))
    h2f = _es.enter_context(nc.sbuf_tensor("h2f", [128, HC * 66], F32))
    h2qb = _es.enter_context(nc.sbuf_tensor("h2qb", [128, HC * 66], BF16))
    i32s = _es.enter_context(nc.sbuf_tensor("i32s", [128, 264], I32))
    tmpf = _es.enter_context(nc.sbuf_tensor("tmpf", [128, 264], F32))
    x2t = _es.enter_context(nc.sbuf_tensor("x2t", [128, DC * X2W], BF16))
    amax1 = tmpf
    wacc = _es.enter_context(nc.sbuf_tensor("wacc", [128, 32], F32))
    ps_s = _es.enter_context(nc.psum_tensor("ps_s", [128, 2048], F32))
    ps_o = _es.enter_context(nc.psum_tensor("ps_o", [128, 512], F32))
    ps_a = _es.enter_context(nc.psum_tensor("ps_a", [128, 512], F32))
    ps_b = _es.enter_context(nc.psum_tensor("ps_b", [128, 512], F32))
    ps_t = _es.enter_context(nc.psum_tensor("ps_t", [128, 1024], BF16))
    h_ld = _es.enter_context(nc.semaphore("s_ld"))
    h_st = _es.enter_context(nc.semaphore("s_st"))
    h_t = _es.enter_context(nc.semaphore("s_t"))
    h_v = _es.enter_context(nc.semaphore("s_v"))
    h_c = _es.enter_context(nc.semaphore("s_c"))
    h_g = _es.enter_context(nc.semaphore("s_g"))
    with nc.Block() as blk:

        sLD, sST = Sem(h_ld), Sem(h_st)
        sT, sV, sC, sG = Sem(h_t), Sem(h_v), Sem(h_c), Sem(h_g)
        ESEM = {"tensor": sT, "vector": sV, "scalar": sC, "gpsimd": sG}

        def _fd(ap):
            try:
                sh = ap.shape
            except Exception:
                return 1 << 20
            n = 1
            for s in sh[1:]:
                n *= int(s)
            return n

        def E(eng, meth, *a, inc=0, bar=None, **kw):
            sem = ESEM.get(eng)
            if bar is None:
                bar = (eng in ("vector", "scalar") and meth != "memset"
                       and len(a) > 0 and _fd(a[0]) < 64)
            if bar and not inc:
                inc = 1
            if inc:
                sem.n += inc

            def f(e, meth=meth, a=a, kw=kw, inc=inc, sem=sem):
                ins = getattr(e, meth)(*a, **kw)
                if inc:
                    ins.then_inc(sem.h, inc)
            Q[eng].append(f)
            if bar:
                Q[eng].append(lambda e, s=sem.h, v=sem.n: e.wait_ge(s, v))
            return sem.n if sem else 0

        def W(eng, sem, val):
            assert val <= sem.n, f"wait-on-future {eng}: {val}>{sem.n}"
            if val > 0:
                Q[eng].append(lambda e, s=sem.h, v=val: e.wait_ge(s, v))

        def MM(out, lhsT, rhs, start, stop, inc=0, tp=None):
            return E("tensor", "matmul", out, lhsT, rhs, start=start,
                     stop=stop, skip_group_check=True, tile_position=tp,
                     inc=inc)

        def dma_in(dst, src):
            prev = sLD.n
            sLD.n += 16

            def f(e, dst=dst, src=src, prev=prev):
                e.wait_ge(sLD.h, prev)
                e.dma_start(dst, src).then_inc(sLD.h, 16)
            Q["sync"].append(f)
            return sLD.n

        def dma_out(dst, src):
            prev = sST.n
            sST.n += 16

            def f(e, dst=dst, src=src, prev=prev):
                e.wait_ge(sST.h, prev)
                e.dma_start(dst, src).then_inc(sST.h, 16)
            Q["sync"].append(f)
            return sST.n

        tapoff = {"f": 0, "b": 0}
        TAPMAP = {}

        def tap(name, ap, width, dt="b"):
            if name not in _TAPS:
                return
            W("sync", sV, sV.n)
            W("sync", sT, sT.n)
            dst_t = dbgf_t if dt == "f" else dbgb_t
            o = tapoff[dt]
            sST.n += 16

            def f(e, dst=AP(dst_t, o, [[32768, 128], [1, width]]), src=ap):
                e.dma_start(dst, src,
                            allow_slow_non_contiguous=True).then_inc(sST.h, 16)
            Q["sync"].append(f)
            TAPMAP[name] = (dt, o, width)
            tapoff[dt] = o + width
            W("vector", sST, sST.n)
            W("scalar", sST, sST.n)
            W("tensor", sST, sST.n)
            return
            TAPMAP[name] = (dt, o, width)
            tapoff[dt] = o + width
        nc._tapmap = TAPMAP

        misc = {"rot": 0, "reader": [0, 0]}

        def acq():
            i = misc["rot"] % 2
            misc["rot"] += 1
            W("tensor", sV, misc["reader"][i])
            return [ps_a, ps_b][i], i

        def rel(i, val):
            misc["reader"][i] = val

        def heat(n):
            bank, bi = acq()
            for _ in range(n):
                MM(bank[:, 0:512], wqb[:, 0:128], wkb[:, 0:512], True, True)
            rel(bi, misc["reader"][bi])

        _RSLOT = {(0, 0): 0, (1, 0): 260, (2, 0): 520, (3, 0): 780,
                  (0, 1): 1040, (0, 2): 1056, (0, 3): 780,
                  (1, 3): 1040, (2, 3): 0, (3, 3): 520,
                  (1, 1): 1040, (2, 1): 1170}

        def rowc(b, j, lo, hi):
            off = _RSLOT[(b, j)]
            return rowbank[0:1, off + lo:off + hi]

        v_fl = 0
        dma_in(constf[:, :], consts_t.ap())
        dma_in(cwbf[:, :], cwb_t.ap())
        v_fl = dma_in(flagf[:, :], flags_t.ap())
        W("vector", sLD, v_fl)
        E("vector", "tensor_copy", identb[:, :], constf[:, 0:128])
        E("vector", "tensor_copy", swapb[:, :], constf[:, 131:259])
        E("vector", "tensor_copy", onesb[:, 0:3], constf[:, 128:131])
        E("vector", "memset",
          AP(v65, 64, [[2 * 16 * VW, 128], [VW, 32], [1, 1]]), 1.0, inc=1)
        v_base = sV.n

        ONES_F = constf[:, 128:129]
        ONES_ROW_F = constf[0:1, 260:388]
        IDENT_F = constf[:, 0:128]
        ONES_B = onesb[:, 0:1]
        HALF0_B = onesb[:, 1:2]
        HALF1_B = onesb[:, 2:3]

        last_stage_user = [v_base]

        wstage = h2f
        _ws_state = {"n": 0, "user": [0, 0]}

        def load_piece(src_t, elem_off, row_stride=1024):
            wi = _ws_state["n"] % 2
            _ws_state["n"] += 1
            W("sync", sV, _ws_state["user"][wi])
            v = dma_in(AP(wstage, 1024 * wi, [[2112, 128], [1, 1024]]),
                       AP(src_t, elem_off, [[row_stride, 128], [1, 1024]]))
            return v, wi

        def stage_slice(wi, lo, hi):
            return wstage[:, 1024 * wi + lo:1024 * wi + hi]

        for src, dst in ((wqT, wqb), (wkT, wkb), (wvT, wvb)):
            for f in range(DC):
                v, wi = load_piece(src, f * 128 * D)
                W("vector", sLD, v)
                _ws_state["user"][wi] = E("vector", "tensor_copy",
                                          dst[:, f * D:(f + 1) * D],
                                          stage_slice(wi, 0, 1024), inc=1)
        v_wcast = max(_ws_state["user"])

        def ternary_prep(srcT, nrow_chunks, ncols, dstq, slot):
            pieces = []
            for k in range(nrow_chunks):
                for cp in range(ncols // 1024):
                    pieces.append((k, cp))
            for pi, (k, cp) in enumerate(pieces):
                v, wi = load_piece(srcT, k * 128 * ncols + cp * 1024, ncols)
                W("vector", sLD, v)
                _ws_state["user"][wi] = E("vector", "tensor_reduce",
                                          wacc[:, pi:pi + 1],
                                          stage_slice(wi, 0, 1024),
                                          AX.X, ALU.add,
                                          apply_absolute_value=True, inc=1)
                if pi % 4 == 3:
                    yield
            npieces = len(pieces)
            bank, bi = acq()
            W("tensor", sV, max(_ws_state["user"]))
            tap(f"wacc_{slot}", wacc[:, :], 32, dt="f")
            g = MM(bank[0:1, 0:npieces], ONES_F, wacc[:, 0:npieces],
                   True, True, inc=1)
            W("vector", sT, g)
            E("vector", "tensor_reduce", rowc(0, 0, 0, 1),
              bank[0:1, 0:npieces], AX.X, ALU.add)
            E("vector", "tensor_scalar", rowc(0, 0, 1, 2), rowc(0, 0, 0, 1),
              1.0 / (HN * D), 1e-5, ALU.mult, ALU.max)
            v = E("vector", "reciprocal", rowc(0, 0, 2, 3), rowc(0, 0, 1, 2),
                  inc=1)
            rel(bi, v)
            tap(f"wsrow_{slot}", rowbank[:, 0:4], 4, dt="f")
            bank, bi = acq()
            W("tensor", sV, v)
            MM(bank[:, 0:1], ONES_ROW_F, rowc(0, 0, 1, 2), True, True)
            g = MM(bank[:, 1:2], ONES_ROW_F, rowc(0, 0, 2, 3), False, True,
                   inc=1)
            W("vector", sT, g)
            v = E("vector", "tensor_copy", smallf[:, slot:slot + 2],
                  bank[:, 0:2], inc=1)
            rel(bi, v)
            for (k, cp) in pieces:
                v, wi = load_piece(srcT, k * 128 * ncols + cp * 1024, ncols)
                W("vector", sLD, v)
                base = k * ncols + cp * 1024
                for s4 in range(4):
                    E("vector", "tensor_scalar", i32s[:, 0:256],
                      stage_slice(wi, 256 * s4, 256 * s4 + 256),
                      smallf[:, slot + 1:slot + 2], 1, ALU.mult, ALU.min)
                    E("vector", "tensor_scalar", i32s[:, 0:256],
                      i32s[:, 0:256], -1, None, ALU.max)
                    E("vector", "tensor_copy",
                      dstq[:, base + 256 * s4:base + 256 * s4 + 256],
                      i32s[:, 0:256],
                      inc=(1 if s4 == 3 else 0))
                _ws_state["user"][wi] = sV.n
                yield
            return sV.n

        wprep_state = {"v": 0}

        def wprep_gen():
            for _ in ternary_prep(w1Tt, DC, HN, w1q, 4):
                yield
            for _ in ternary_prep(w2Tt, HC, D, w2q, 6):
                yield
            wprep_state["v"] = sV.n
            yield


        def ln_stats(src, stride, tok):
            E("vector", "tensor_tensor",
              AP(sqb, 0, [[DC * 260, 128], [260, DC], [1, tok]]),
              AP(src, 0, [[src.shape[1], 128], [stride, DC], [1, tok]]),
              AP(src, 0, [[src.shape[1], 128], [stride, DC], [1, tok]]),
              ALU.mult, inc=1)
            v_sq = sV.n
            bankA, biA = acq()
            bankB, biB = acq()
            W("tensor", sV, v_sq)
            for f in range(DC):
                MM(bankA[0:1, 0:tok], ONES_B,
                   src[:, f * stride:f * stride + tok], f == 0, f == DC - 1)
            for f in range(DC):
                MM(bankB[0:1, 0:tok], ONES_B, sqb[:, f * 260:f * 260 + tok],
                   f == 0, f == DC - 1, inc=(1 if f == DC - 1 else 0))
            g = sT.n
            W("vector", sT, g)
            E("vector", "tensor_scalar", rowc(0, 0, 0, tok), bankA[0:1, 0:tok],
              1.0 / D, None, ALU.mult)
            E("vector", "tensor_scalar", rowc(3, 0, 0, tok), bankB[0:1, 0:tok],
              1.0 / D, None, ALU.mult)
            E("vector", "tensor_tensor", rowc(1, 0, 0, tok), rowc(0, 0, 0, tok),
              rowc(0, 0, 0, tok), ALU.mult)
            v = E("vector", "tensor_tensor", rowc(3, 0, 0, tok),
                  rowc(3, 0, 0, tok), rowc(1, 0, 0, tok), ALU.subtract, inc=1)
            rel(biA, v)
            rel(biB, v)
            W("scalar", sV, v)
            E("scalar", "activation", rowc(3, 0, 0, tok), rowc(3, 0, 0, tok),
              ACTF.Ln, bias=constf[0:1, 388:389], scale=1.0)
            c = E("scalar", "activation", rowc(1, 0, 0, tok),
                  rowc(3, 0, 0, tok), ACTF.Exp, bias=0.0, scale=-0.5, inc=1)
            W("vector", sC, c)
            return E("vector", "tensor_tensor", rowc(2, 0, 0, tok),
                     rowc(0, 0, 0, tok), rowc(1, 0, 0, tok), ALU.mult, inc=1)

        def replicate(row_ap, col, tok, after_v=0):
            assert tok <= 512
            bank, bi = acq()
            W("tensor", sV, after_v)
            g = MM(bank[:, 0:tok], ONES_ROW_F, row_ap, True, True, inc=1)
            W("vector", sT, g)
            v = E("vector", "tensor_copy", reps[:, col:col + tok],
                  bank[:, 0:tok], inc=1)
            rel(bi, v)
            return v

        def rep_b(col, n, w, off=0):
            return AP(reps, col + off, [[896, 128], [0, n], [1, w]])

        def proj(wb, src, dst, v_src, w_ready):
            out = 0
            for m0 in range(0, DC, 2):
                bank, bi = acq()
                W("tensor", sV, max(v_src, w_ready))
                for j in range(2):
                    m = m0 + j
                    for k in range(DC):
                        MM(bank[:, 256 * j:256 * j + BT],
                           wb[:, D * k + 128 * m:D * k + 128 * m + 128],
                           src[:, k * BT:k * BT + BT],
                           (j == 0 and k == 0), k == DC - 1,
                           inc=(1 if (j == 1 and k == DC - 1) else 0))
                g = sT.n
                W("vector", sT, g)
                out = E("vector", "tensor_copy",
                        dst[:, m0 * BT:m0 * BT + 512],
                        bank[:, 0:512], inc=1)
                rel(bi, out)
            return out

        def proj_v(src, v_src, w_ready):
            out = 0
            for tt in range(2):
                for half in range(2):
                    bank, bi = acq()
                    W("tensor", sV, max(v_src, w_ready))
                    for k in range(DC):
                        MM(bank[:, 0:512],
                           src[:, k * BT + 128 * tt:k * BT + 128 * tt + 128],
                           wvb[:, D * k + 512 * half:D * k + 512 * half + 512],
                           k == 0, k == DC - 1, inc=(1 if k == DC - 1 else 0))
                    g = sT.n
                    W("vector", sT, g)
                    out = E("vector", "tensor_copy",
                            AP(v65, (16 * VW) * tt + (8 * half) * VW,
                               [[2 * 16 * VW, 128], [VW, 8], [1, 64]]),
                            AP(bank, 0, [[512, 128], [64, 8], [1, 64]]),
                            inc=1)
                    rel(bi, out)
            return out

        wo_state = {"cast": [0, 0], "user": [0, 0], "cast_last": 0}
        wosf = h2f

        def proj_wo(v_src):
            out = 0
            for m in range(DC):
                wi = m % 2
                wos = [wos0, wos1][wi]
                W("sync", sV, max(wo_state["cast_last"],
                                  _ws_state["user"][1]))
                v = dma_in(AP(wosf, 1024, [[2112, 128], [128, DC], [1, 128]]),
                           AP(woT, 128 * m, [[D, 128], [128 * D, DC], [1, 128]]))
                W("vector", sLD, v)
                W("vector", sT, wo_state["user"][wi])
                wo_state["cast"][wi] = E("vector", "tensor_copy", wos[:, :],
                                         wosf[:, 1024:2048], inc=1)
                wo_state["cast_last"] = wo_state["cast"][wi]
                _ws_state["user"][1] = wo_state["cast"][wi]
                bank, bi = acq()
                W("tensor", sV, max(v_src, wo_state["cast"][wi]))
                for k in range(DC):
                    MM(bank[:, 0:BT], wos[:, 128 * k:128 * k + 128],
                       xn[:, k * BT:k * BT + BT],
                       k == 0, k == DC - 1, inc=(1 if k == DC - 1 else 0))
                g = sT.n
                wo_state["user"][wi] = g
                W("vector", sT, g)
                out = E("vector", "tensor_copy", ptb[:, m * BT:m * BT + BT],
                        bank[:, 0:BT], inc=1)
                rel(bi, out)
            return out

        def phase_ln1_qkv(xbuf, v_x):
            W("vector", sLD, v_x)
            E("vector", "tensor_copy", xn[:, :], xbuf[:, :], inc=1)
            v_ln = ln_stats(xn, BT, BT)
            v1 = replicate(rowc(1, 0, 0, BT), 0, BT, after_v=v_ln)
            v2 = replicate(rowc(2, 0, 0, BT), 256, BT, after_v=v_ln)
            W("vector", sV, max(v1, v2))
            E("vector", "tensor_tensor",
              AP(sqb, 0, [[DC * 260, 128], [260, DC], [1, BT]]),
              AP(xbuf, 0, [[DC * BT, 128], [BT, DC], [1, BT]]),
              rep_b(0, DC, BT), ALU.mult)
            return E("vector", "tensor_tensor",
                     AP(xn, 0, [[DC * BT, 128], [BT, DC], [1, BT]]),
                     AP(sqb, 0, [[DC * 260, 128], [260, DC], [1, BT]]),
                     rep_b(256, DC, BT), ALU.subtract, inc=1)

        def phase_bound(v_q, v_k, c_prev_exp):
            for (xt, vv, col) in ((xq, v_q, 0), (xk, v_k, 1)):
                W("vector", sV, vv)
                E("vector", "tensor_tensor",
                  AP(sqb, 0, [[DC * 260, 128], [260, DC], [1, BT]]),
                  AP(xt, 0, [[DC * BT, 128], [BT, DC], [1, BT]]),
                  AP(xt, 0, [[DC * BT, 128], [BT, DC], [1, BT]]),
                  ALU.mult, inc=1)
                v_sq = sV.n
                v_red = 0
                for rd in range(2):
                    W("tensor", sV, max(v_sq, v_red))
                    W("tensor", sC, c_prev_exp)
                    for sl in range(8):
                        f, hf = (8 * rd + sl) // 2, (8 * rd + sl) % 2
                        MM(ps_s[0:1, sl * 256:sl * 256 + BT],
                           HALF0_B if hf == 0 else HALF1_B,
                           sqb[:, f * 260:f * 260 + BT],
                           True, True, inc=(1 if sl == 7 else 0))
                    g = sT.n
                    W("vector", sT, g)
                    v_red = E("vector", "tensor_reduce",
                              rowc(0, 1, 4 + rd, 5 + rd),
                              ps_s[0:1, 0:2048], AX.X, ALU.max, inc=1)
                E("vector", "tensor_tensor", rowc(0, 1, col, col + 1),
                  rowc(0, 1, 4, 5), rowc(0, 1, 5, 6), ALU.max, inc=1)
            E("vector", "tensor_tensor", rowc(0, 1, 2, 3), rowc(0, 1, 0, 1),
              rowc(0, 1, 1, 2), ALU.mult, inc=1)
            v_bb = sV.n
            W("scalar", sV, v_bb)
            E("scalar", "activation", rowc(0, 1, 6, 7), rowc(0, 1, 2, 3),
              ACTF.Ln, bias=constf[0:1, 389:390], scale=1.0)
            c = E("scalar", "activation", rowc(0, 1, 7, 8), rowc(0, 1, 6, 7),
                  ACTF.Exp, bias=0.0, scale=0.5, inc=1)
            W("vector", sC, c)
            E("vector", "tensor_scalar", rowc(0, 1, 8, 9), rowc(0, 1, 7, 8),
              -SCALE, -1.0, ALU.mult, ALU.add, inc=1)
            v_brow = sV.n
            bank, bi = acq()
            W("tensor", sV, v_brow)
            g = MM(bank[:, 0:1], ONES_ROW_F, rowc(0, 1, 8, 9), True, True,
                   inc=1)
            W("vector", sT, g)
            v = E("vector", "tensor_copy", smallf[:, 0:1], bank[:, 0:1],
                  inc=1)
            rel(bi, v)
            return v

        att = {"pt_user": 0, "ao_evac": 0,
               "exp_half": [0, 0], "pt_half": [0, 0]}

        def phase_attention(v_q, v_k, v_v, v_bias, dst_ao, inject=None):
            c_exp = 0
            v_round = 0
            exp_half = att["exp_half"]
            pt_half = att["pt_half"]

            def emit_S(r, p, v_xqs, qb=0):
                hp = p % 2
                W("tensor", sC, exp_half[hp])
                for jj in range(2):
                    jt = 2 * p + jj
                    cj, sh = jt // 2, jt % 2
                    pe = cj % 2
                    lhs = xk[64 * pe:64 * pe + 64,
                             (cj // 2) * BT + 128 * sh:
                             (cj // 2) * BT + 128 * sh + 128]
                    rhs = qrhs[64 * pe:64 * pe + 64,
                               qb + 256 * pe:qb + 256 * pe + 512]
                    MM(ps_s[:, 1024 * hp + 512 * jj:
                            1024 * hp + 512 * jj + 512],
                       lhs, rhs, True, True,
                       inc=(1 if jj == 1 else 0))
                return sT.n

            def emit_exp(p, g_s):
                nonlocal c_exp
                hp = p % 2
                W("scalar", sT, g_s)
                W("scalar", sT, pt_half[hp])
                c_exp = E("scalar", "activation",
                          ptb[:, 1024 * hp:1024 * hp + 1024],
                          ps_s[:, 1024 * hp:1024 * hp + 1024], ACTF.Exp,
                          bias=smallf[:, 0:1], scale=SCALE, inc=1)
                exp_half[hp] = c_exp
                return c_exp

            def emit_PV(p, c_e, first):
                hp = p % 2
                W("tensor", sC, c_e)
                for jj in range(2):
                    jt = 2 * p + jj
                    cj, sh = jt // 2, jt % 2
                    vsl = v65[:, (16 * VW) * sh + VW * cj:
                              (16 * VW) * sh + VW * cj + 65]
                    for s in range(4):
                        MM(ps_o[:, 128 * s:128 * s + 65],
                           ptb[:, 1024 * hp + 512 * jj + 128 * s:
                               1024 * hp + 512 * jj + 128 * s + 128],
                           vsl, first, True,
                           inc=(1 if (jj == 1 and s == 3) else 0))
                        first = False
                pt_half[hp] = sT.n
                return sT.n

            def prep_qrhs(r):
                qb = 768 * (r % 2)
                bank, bi = acq()
                W("tensor", sV, v_q)
                W("tensor", sT, att["pt_user"])
                g = MM(bank[:, 0:BT], swapb[:, :], xq[:, r * BT:r * BT + BT],
                       True, True, inc=1)
                W("vector", sT, g)
                E("vector", "tensor_copy", qrhs[:, qb + 256:qb + 512],
                  bank[:, 0:BT])
                E("vector", "tensor_copy", qrhs[:, qb:qb + 256],
                  xq[:, r * BT:r * BT + BT])
                v = E("vector", "tensor_copy", qrhs[:, qb + 512:qb + 768],
                      xq[:, r * BT:r * BT + BT], inc=1)
                rel(bi, v)
                return v

            def epilogue(r):
                nonlocal v_round
                W("vector", sT, att["pt_user"])
                for s in range(4):
                    E("vector", "reciprocal", rcol[:, s:s + 1],
                      ps_o[:, 128 * s + 64:128 * s + 65])
                for s in range(4):
                    a, u = s // 2, s % 2
                    E("vector", "tensor_scalar",
                      stage[:, 128 * u + 64 * a:128 * u + 64 * a + 64],
                      ps_o[:, 128 * s:128 * s + 64],
                      rcol[:, s:s + 1], None, ALU.mult,
                      inc=(1 if s == 3 else 0))
                v_round = sV.n
                W("tensor", sV, v_round)
                W("tensor", sV, att["ao_evac"] if att["ao_evac"] else v_base)
                for u in range(2):
                    E("tensor", "transpose", ps_t[:, 128 * u:128 * u + 128],
                      stage[:, 128 * u:128 * u + 128], identb[:, :],
                      inc=(1 if u == 1 else 0))
                g_tr = sT.n
                W("vector", sT, g_tr)
                for u in range(2):
                    E("vector", "tensor_copy",
                      dst_ao[:, r * BT + 128 * u:r * BT + 128 * u + 128],
                      ps_t[:, 128 * u:128 * u + 128],
                      inc=(1 if u == 1 else 0))
                att["ao_evac"] = sV.n

            v_xqs_by = {}
            v_xqs_by[0] = prep_qrhs(0)
            for r in range(NB):
                if r + 1 < NB:
                    v_xqs_by[r + 1] = prep_qrhs(r + 1)
                v_xqs = v_xqs_by[r]
                qb = 768 * (r % 2)
                W("scalar", sV, v_bias)
                if v_round:
                    W("tensor", sV, v_round)
                W("tensor", sV, max(v_xqs, v_k, v_bias, v_v))
                gs = {}
                ce = {}
                for p in range(16):
                    gs[p] = emit_S(r, p, v_xqs, qb)
                    if p >= 1:
                        ce[p - 1] = emit_exp(p - 1, gs[p - 1])
                        emit_PV(p - 1, ce[p - 1], first=(p - 1 == 0))
                    if inject is not None:
                        next(inject, None)
                ce[15] = emit_exp(15, gs[15])
                emit_PV(15, ce[15], first=False)
                att["pt_user"] = sT.n
                epilogue(r)
                if inject is not None:
                    next(inject, None)
            return c_exp, att["ao_evac"]

        v_xh = dma_in(AP(x1, 0, [[DC * BT, 128], [BT, DC], [1, BT]]),
                      AP(xTh, 0, [[BT, 128], [128 * BT, DC], [1, BT]]))
        v_xn_h = phase_ln1_qkv(x1, v_xh)
        v_qh = proj(wqb, xn, xq, v_xn_h, v_wcast)
        v_kh = proj(wkb, xn, xk, v_xn_h, v_wcast)
        v_vh = proj_v(xn, v_xn_h, max(v_wcast, v_base))
        v_bias_h = phase_bound(v_qh, v_kh, 0)

        W("vector", sV, v_qh)
        E("vector", "tensor_copy",
          AP(qc, 0, [[16, 128], [1, 8], [1, 1]]),
          AP(xq, 0, [[DC * BT, 128], [BT, 8], [1, 1]]), inc=1)
        v_qc = sV.n
        bank, bi = acq()
        W("tensor", sV, v_qc)
        g = MM(bank[:, 0:8], swapb[:, :], qc[:, 0:8], True, True, inc=1)
        W("vector", sT, g)
        v_qcs = E("vector", "tensor_copy", qc[:, 8:16], bank[:, 0:8], inc=1)
        rel(bi, v_qcs)

        W("tensor", sV, max(v_qcs, v_kh, v_bias_h))
        for jt in range(32):
            cj, sh = jt // 2, jt % 2
            base = 64 * (cj % 2)
            lhs = xk[base:base + 64,
                     (cj // 2) * BT + 128 * sh:(cj // 2) * BT + 128 * sh + 128]
            for ci in range(16):
                f = ci // 2
                col = f if (ci % 2) == (cj % 2) else 8 + f
                MM(ps_s[:, 16 * jt + ci:16 * jt + ci + 1],
                   lhs, qc[base:base + 64, col:col + 1],
                   (jt == 0 and ci == 0), True,
                   inc=(1 if (jt == 31 and ci == 15) else 0))
        g_sh = sT.n
        W("scalar", sT, g_sh)
        W("scalar", sV, v_bias_h)
        c_he = E("scalar", "activation", ptb[:, 0:512], ps_s[:, 0:512],
                 ACTF.Exp, bias=smallf[:, 0:1], scale=SCALE, inc=1)
        W("tensor", sC, c_he)
        W("tensor", sV, v_vh)
        for jt in range(32):
            cj, sh = jt // 2, jt % 2
            vsl = v65[:, (16 * VW) * sh + VW * cj:(16 * VW) * sh + VW * cj + 65]
            MM(ps_o[0:16, 0:65], ptb[:, 16 * jt:16 * jt + 16], vsl,
               jt == 0, jt == 31, inc=(1 if jt == 31 else 0))
        g_pv = sT.n
        W("vector", sT, g_pv)
        E("vector", "reciprocal", rcol[0:16, 0:1], ps_o[0:16, 64:65])
        v_on = E("vector", "tensor_scalar", onorm[0:16, 0:64],
                 ps_o[0:16, 0:64], rcol[0:16, 0:1], None, ALU.mult, inc=1)
        W("sync", sV, v_on)
        v_in2 = dma_in(AP(in2, 0, [[128, 8], [64, 2], [1, 64]]),
                       AP(onorm, 0, [[64, 16], [1, 64]]))
        W("tensor", sLD, v_in2)
        g_ao = E("tensor", "transpose", ps_t[:, 0:8], in2[0:8, :],
                 identb[0:8, 0:8], inc=1)
        W("vector", sT, g_ao)
        v_aoh = E("vector", "tensor_copy", hcol[:, 0:8], ps_t[:, 0:8], inc=1)
        bank, bi = acq()
        for m in range(DC):
            wi = m % 2
            wos = [wos0, wos1][wi]
            W("sync", sV, max(wo_state["cast_last"], _ws_state["user"][1]))
            v = dma_in(AP(wosf, 1024, [[2112, 128], [128, DC], [1, 128]]),
                       AP(woT, 128 * m, [[D, 128], [128 * D, DC], [1, 128]]))
            W("vector", sLD, v)
            W("vector", sT, wo_state["user"][wi])
            wo_state["cast"][wi] = E("vector", "tensor_copy", wos[:, :],
                                     wosf[:, 1024:2048], inc=1)
            wo_state["cast_last"] = wo_state["cast"][wi]
            _ws_state["user"][1] = wo_state["cast"][wi]
            W("tensor", sV, max(v_aoh, wo_state["cast_last"]))
            for k in range(DC):
                MM(bank[:, m:m + 1], wos[:, 128 * k:128 * k + 128],
                   hcol[:, k:k + 1], (m == 0 and k == 0), k == DC - 1,
                   inc=(1 if k == DC - 1 else 0))
            wo_state["user"][wi] = sT.n
        g_wo = sT.n
        W("vector", sT, g_wo)
        v_aoh = E("vector", "tensor_copy", hcol[:, 8:16], bank[:, 0:8],
                  inc=1)
        rel(bi, v_aoh)

        def col_ln(src_bf8, sq_bf8, vv):
            E("vector", "tensor_tensor", sq_bf8, src_bf8, src_bf8, ALU.mult,
              inc=1)
            v_s = max(vv, sV.n)
            bank, bi = acq()
            W("tensor", sV, v_s)
            MM(bank[0:1, 0:8], ONES_B, src_bf8, True, True)
            g = MM(bank[0:1, 8:16], ONES_B, sq_bf8, False, True, inc=1)
            W("vector", sT, g)
            E("vector", "tensor_reduce", rowc(0, 2, 0, 1), bank[0:1, 0:8],
              AX.X, ALU.add)
            E("vector", "tensor_reduce", rowc(0, 2, 1, 2), bank[0:1, 8:16],
              AX.X, ALU.add, inc=1)
            rel(bi, sV.n)
            E("vector", "tensor_scalar", rowc(0, 2, 0, 1), rowc(0, 2, 0, 1),
              1.0 / D, None, ALU.mult)
            E("vector", "tensor_scalar", rowc(0, 2, 1, 2), rowc(0, 2, 1, 2),
              1.0 / D, None, ALU.mult)
            E("vector", "tensor_tensor", rowc(0, 2, 2, 3), rowc(0, 2, 0, 1),
              rowc(0, 2, 0, 1), ALU.mult)
            v = E("vector", "tensor_tensor", rowc(0, 2, 1, 2),
                  rowc(0, 2, 1, 2), rowc(0, 2, 2, 3), ALU.subtract, inc=1)
            W("scalar", sV, v)
            E("scalar", "activation", rowc(0, 2, 1, 2), rowc(0, 2, 1, 2),
              ACTF.Ln, bias=constf[0:1, 388:389], scale=1.0)
            c = E("scalar", "activation", rowc(0, 2, 1, 2), rowc(0, 2, 1, 2),
                  ACTF.Exp, bias=0.0, scale=-0.5, inc=1)
            bank, bi = acq()
            W("tensor", sC, c)
            MM(bank[:, 0:1], ONES_ROW_F, rowc(0, 2, 0, 1), True, True)
            g = MM(bank[:, 1:2], ONES_ROW_F, rowc(0, 2, 1, 2), False, True,
                   inc=1)
            W("vector", sT, g)
            v = E("vector", "tensor_copy", smallf[:, 8:10], bank[:, 0:2],
                  inc=1)
            rel(bi, v)
            return v

        v_ln3h = col_ln(hcol[:, 8:16], hcol[:, 24:32], v_aoh)
        W("vector", sV, v_ln3h)
        E("vector", "tensor_tensor", smallf[:, 10:11], smallf[:, 8:9],
          smallf[:, 9:10], ALU.mult)
        E("vector", "scalar_tensor_tensor", tmpf[:, 0:8], hcol[:, 8:16],
          smallf[:, 9:10],
          AP(x1, 0, [[DC * BT, 128], [BT, 8], [1, 1]]), ALU.mult, ALU.add)
        v_xmh = E("vector", "tensor_scalar", tmpf[:, 0:8], tmpf[:, 0:8],
                  smallf[:, 10:11], None, ALU.subtract, inc=1)

        hstate = {"v_carry": 0, "v_h1s": 0}
        def halo_part2():
            W("vector", sV, v_xmh)
            v_xmb = E("vector", "tensor_copy", hcol[:, 16:24], tmpf[:, 0:8],
                      inc=1)
            v_ln2h = col_ln(hcol[:, 16:24], hcol[:, 24:32], v_xmb)
            W("vector", sV, v_ln2h)
            E("vector", "tensor_scalar", tmpf[:, 8:16], tmpf[:, 0:8],
              smallf[:, 8:9], None, ALU.subtract)
            E("vector", "tensor_reduce", rcol[:, 0:1], tmpf[:, 8:16],
              AX.X, ALU.max, apply_absolute_value=True, inc=1)
            v_am = sV.n
            bank, bi = acq()
            W("tensor", sV, v_am)
            g = E("tensor", "transpose", bank[0:1, 0:128], rcol[:, 0:1], IDENT_F,
                  inc=1)
            W("vector", sT, g)
            E("vector", "tensor_reduce", rowc(0, 2, 3, 4), bank[0:1, 0:128],
              AX.X, ALU.max)
            E("vector", "tensor_tensor", rowc(0, 2, 3, 4), rowc(0, 2, 3, 4),
              rowc(0, 2, 1, 2), ALU.mult)
            E("vector", "tensor_scalar", rowc(0, 2, 3, 4), rowc(0, 2, 3, 4),
              1e-5, None, ALU.max)
            E("vector", "reciprocal", rowc(0, 2, 4, 5), rowc(0, 2, 3, 4))
            E("vector", "tensor_scalar", rowc(0, 2, 4, 5), rowc(0, 2, 4, 5),
              127.0, None, ALU.mult)
            E("vector", "tensor_tensor", rowc(0, 2, 5, 6), rowc(0, 2, 1, 2),
              rowc(0, 2, 4, 5), ALU.mult)
            E("vector", "tensor_tensor", rowc(0, 2, 6, 7), rowc(0, 2, 0, 1),
              rowc(0, 2, 5, 6), ALU.mult)
            E("vector", "tensor_scalar", rowc(0, 2, 7, 8), rowc(0, 2, 3, 4),
              smallf[0:1, 4:5], 1.0 / 127.0, ALU.mult, ALU.mult, inc=1)
            v_rows_h = sV.n
            rel(bi, v_rows_h)
            bank, bi = acq()
            W("tensor", sV, v_rows_h)
            MM(bank[:, 0:1], ONES_ROW_F, rowc(0, 2, 5, 6), True, True)
            MM(bank[:, 1:2], ONES_ROW_F, rowc(0, 2, 6, 7), False, True)
            g = MM(bank[:, 2:3], ONES_ROW_F, rowc(0, 2, 7, 8), False, True, inc=1)
            W("vector", sT, g)
            v = E("vector", "tensor_copy", smallf[:, 11:14], bank[:, 0:3], inc=1)
            rel(bi, v)
            W("vector", sV, v)
            E("vector", "tensor_scalar", i32s[:, 0:8], tmpf[:, 0:8],
              smallf[:, 11:12], smallf[:, 12:13], ALU.mult, ALU.subtract)
            v_q1h = E("vector", "tensor_copy", qc[:, 0:8], i32s[:, 0:8], inc=1)

            bank, bi = acq()
            W("tensor", sV, max(v_q1h, wprep_state["v"]))
            for m in range(HC):
                for k in range(DC):
                    MM(bank[:, m:m + 1],
                       w1q[:, HN * k + 128 * m:HN * k + 128 * m + 128],
                       qc[:, k:k + 1], (m == 0 and k == 0), k == DC - 1,
                       inc=(1 if (m == HC - 1 and k == DC - 1) else 0))
            g = sT.n
            W("vector", sT, g)
            E("vector", "tensor_scalar", h1L[:, 0:HC], bank[:, 0:HC],
              smallf[:, 13:14], None, ALU.mult)
            E("vector", "tensor_scalar", h1R[:, 0:HC], h1L[:, 0:HC],
              flagf[:, 1:2], None, ALU.mult)
            hstate["v_h1s"] = E("vector", "tensor_scalar",
                                h1L[:, 0:HC], h1L[:, 0:HC],
                                flagf[:, 0:1], None, ALU.mult, inc=1)
            rel(bi, hstate["v_h1s"])
            tap("h1L", h1L[:, :], HC)
            tap("h1R", h1R[:, :], HC)
            E("vector", "memset", carry_h1[:, :], 0.0)
            hstate["v_carry"] = E("vector", "tensor_copy",
                        AP(carry_h1, 1, [[2 * HC, 128], [2, HC], [1, 1]]),
                        AP(h1L, 0, [[HC, 128], [1, HC], [1, 1]]), inc=1)


        def make_tail(h, xb, v_h1ready):
            ws_g, TW = _win(h)
            inb = 1 if h == 0 else 0
            halves = [(0, min(128, TW))]
            if TW > 128:
                halves.append((128, TW - 128))
            v_x2 = 0
            for (hoff, HW) in halves:
                W("vector", sC, tail_state["c_si"])
                for c in range(HC):
                    b0 = H1W * c + inb + hoff
                    E("vector", "tensor_scalar", yb[:, YW * c:YW * c + HW],
                      h1b[:, b0:b0 + HW],
                      cwbf[:, 3 * c:3 * c + 1], cwbf[:, 96 + c:97 + c],
                      ALU.mult, ALU.add)
                    E("vector", "scalar_tensor_tensor",
                      yb[:, YW * c:YW * c + HW], h1b[:, b0 + 1:b0 + 1 + HW],
                      cwbf[:, 3 * c + 1:3 * c + 2], yb[:, YW * c:YW * c + HW],
                      ALU.mult, ALU.add)
                    E("vector", "scalar_tensor_tensor",
                      yb[:, YW * c:YW * c + HW], h1b[:, b0 + 2:b0 + 2 + HW],
                      cwbf[:, 3 * c + 2:3 * c + 3], yb[:, YW * c:YW * c + HW],
                      ALU.mult, ALU.add, inc=(1 if c % 4 == 3 else 0))
                    if c % 4 == 3:
                        yield
                v_conv = sV.n
                tail_state["g_conv"] = 0
                quarters = [(0, min(64, HW))]
                if HW > 64:
                    quarters.append((64, HW - 64))
                for (qoff, QW) in quarters:
                    if h == 0 and hoff == 0 and qoff == 0:
                        tap("yb_q0", yb[:, :], HC * YW)
                    W("scalar", sV, max(v_conv, tail_state["v_h2q"]))
                    c_si = E("scalar", "activation",
                             AP(h2f, 0, [[HC * 66, 128], [66, HC], [1, QW]]),
                             AP(yb, qoff, [[HC * YW, 128], [YW, HC], [1, QW]]),
                             ACTF.Silu, inc=1)
                    tail_state["c_si"] = c_si
                    W("vector", sC, c_si)
                    E("vector", "tensor_reduce", amax1[:, 0:QW],
                      AP(h2f, 0, [[HC * 66, 128], [1, QW], [66, HC]]),
                      AX.X, ALU.max, apply_absolute_value=True, inc=1)
                    v_amq = sV.n
                    bank, bi = acq()
                    W("tensor", sV, v_amq)
                    g = E("tensor", "transpose", bank[0:QW, 0:128],
                          amax1[:, 0:QW], IDENT_F, inc=1)
                    W("vector", sT, g)
                    E("vector", "tensor_reduce", rcol[0:QW, 4:5],
                      bank[0:QW, 0:128], AX.X, ALU.max, inc=1)
                    v_s2c = sV.n
                    rel(bi, v_s2c)
                    bank, bi = acq()
                    W("tensor", sV, v_s2c)
                    g = E("tensor", "transpose", bank[0:1, 0:QW],
                          rcol[0:QW, 4:5], IDENT_F[0:QW, 0:QW], inc=1)
                    W("vector", sT, g)
                    E("vector", "tensor_scalar", rowc(1, 1, 0, QW),
                      bank[0:1, 0:QW], 1e-5, None, ALU.max)
                    E("vector", "tensor_scalar", rowc(2, 1, 0, QW),
                      rowc(1, 1, 0, QW), smallf[0:1, 6:7], 1.0 / 127.0,
                      ALU.mult, ALU.mult)
                    E("vector", "reciprocal", rowc(1, 1, 0, QW),
                      rowc(1, 1, 0, QW))
                    E("vector", "tensor_scalar", rowc(1, 1, 0, QW),
                      rowc(1, 1, 0, QW), 127.0, None, ALU.mult, inc=1)
                    v = sV.n
                    rel(bi, v)
                    vs2 = replicate(rowc(1, 1, 0, QW), 0, QW, after_v=v)
                    va2 = replicate(rowc(2, 1, 0, QW), 128, QW, after_v=v)
                    W("vector", sV, max(vs2, va2))
                    yield
                    for g8 in range(8):
                        E("vector", "tensor_tensor",
                          AP(i32s, 0, [[264, 128], [QW, 4], [1, QW]]),
                          AP(h2f, 66 * 4 * g8,
                             [[HC * 66, 128], [66, 4], [1, QW]]),
                          rep_b(0, 4, QW), ALU.mult)
                        E("vector", "tensor_copy",
                          AP(h2qb, 66 * 4 * g8,
                             [[HC * 66, 128], [66, 4], [1, QW]]),
                          AP(i32s, 0, [[264, 128], [QW, 4], [1, QW]]),
                          inc=(1 if g8 % 4 == 3 else 0))
                        if g8 == 3:
                            yield
                    v_h2q = sV.n
                    tail_state["v_h2q"] = v_h2q
                    yield
                    if h == 0 and hoff == 0 and qoff == 0:
                        tap("h2f_q0", h2f[:, :], HC * 66, dt="f")
                        tap("h2q_q0", h2qb[:, :], HC * 66)
                        tap("reps_q0", reps[:, 0:256], 256, dt="f")
                    for m0 in range(0, DC, 4):
                        bank, bi = acq()
                        W("tensor", sV, max(v_h2q, wprep_state["v"]))
                        for j in range(4):
                            m = m0 + j
                            for k in range(HC):
                                MM(bank[:, 128 * j:128 * j + QW],
                                   w2q[:, D * k + 128 * m:
                                       D * k + 128 * m + 128],
                                   h2qb[:, 66 * k:66 * k + QW],
                                   (j == 0 and k == 0), k == HC - 1,
                                   inc=(1 if (j == 3 and k == HC - 1) else 0))
                            if j == 1:
                                yield
                        g = sT.n
                        W("vector", sT, g)
                        v_x2 = E("vector", "tensor_tensor",
                                 AP(x2t, X2W * m0 + hoff + qoff,
                                    [[DC * X2W, 128], [X2W, 4], [1, QW]]),
                                 AP(bank, 0, [[512, 128], [128, 4], [1, QW]]),
                                 rep_b(128, 4, QW), ALU.mult, inc=1)
                        rel(bi, v_x2)
                        yield
            tap(f"x2_{h}", x2t[:, :], DC * X2W)
            W("vector", sV, v_x2)
            E("vector", "tensor_tensor",
              AP(sqb, 0, [[DC * 260, 128], [260, DC], [1, TW]]),
              AP(x2, 0, [[DC * X2W, 128], [X2W, DC], [1, TW]]),
              AP(x2, 0, [[DC * X2W, 128], [X2W, DC], [1, TW]]),
              ALU.mult, inc=1)
            v_sq4 = sV.n
            yield
            bankA, biA = acq()
            bankB, biB = acq()
            W("tensor", sV, v_sq4)
            for f in range(DC):
                MM(bankA[0:1, 0:TW], ONES_B, x2[:, f * X2W:f * X2W + TW],
                   f == 0, f == DC - 1)
            for f in range(DC):
                MM(bankB[0:1, 0:TW], ONES_B, sqb[:, f * 260:f * 260 + TW],
                   f == 0, f == DC - 1, inc=(1 if f == DC - 1 else 0))
            g = sT.n
            W("vector", sT, g)
            E("vector", "tensor_scalar", rowc(0, 0, 0, TW), bankA[0:1, 0:TW],
              1.0 / D, None, ALU.mult)
            E("vector", "tensor_scalar", rowc(3, 0, 0, TW), bankB[0:1, 0:TW],
              1.0 / D, None, ALU.mult)
            E("vector", "tensor_tensor", rowc(1, 0, 0, TW), rowc(0, 0, 0, TW),
              rowc(0, 0, 0, TW), ALU.mult)
            v = E("vector", "tensor_tensor", rowc(3, 0, 0, TW),
                  rowc(3, 0, 0, TW), rowc(1, 0, 0, TW), ALU.subtract, inc=1)
            rel(biA, v)
            rel(biB, v)
            W("scalar", sV, v)
            E("scalar", "activation", rowc(3, 0, 0, TW), rowc(3, 0, 0, TW),
              ACTF.Ln, bias=constf[0:1, 388:389], scale=1.0)
            c4 = E("scalar", "activation", rowc(1, 0, 0, TW),
                   rowc(3, 0, 0, TW), ACTF.Exp, bias=0.0, scale=-0.5, inc=1)
            W("vector", sC, c4)
            E("vector", "tensor_tensor", rowc(2, 0, 0, TW), rowc(0, 0, 0, TW),
              rowc(1, 0, 0, TW), ALU.mult, inc=1)
            v_ln4 = sV.n
            v1 = replicate(rowc(1, 0, 0, TW), 0, TW, after_v=v_ln4)
            v2 = replicate(rowc(2, 0, 0, TW), 320, TW, after_v=v_ln4)
            W("vector", sV, max(v1, v2))
            yield
            if h == 0:
                E("vector", "tensor_tensor",
                  AP(sqb, 0, [[DC * 260, 128], [260, DC], [1, TW]]),
                  AP(x2, 0, [[DC * X2W, 128], [X2W, DC], [1, TW]]),
                  rep_b(0, DC, TW), ALU.mult)
                E("vector", "tensor_tensor",
                  AP(sqb, 0, [[DC * 260, 128], [260, DC], [1, TW]]),
                  AP(sqb, 0, [[DC * 260, 128], [260, DC], [1, TW]]),
                  rep_b(320, DC, TW), ALU.subtract, inc=1)
                yield
                v_out = E("vector", "tensor_tensor",
                          AP(xb, 0, [[DC * BT, 128], [BT, DC], [1, TW]]),
                          AP(xb, 0, [[DC * BT, 128], [BT, DC], [1, TW]]),
                          AP(sqb, 0, [[DC * 260, 128], [260, DC], [1, TW]]),
                          ALU.add, inc=1)
            else:
                E("vector", "tensor_tensor", tmpf[:, 0:DC],
                  AP(x2, 0, [[DC * X2W, 128], [X2W, DC], [1, 1]]),
                  rep_b(0, DC, 1), ALU.mult)
                E("vector", "tensor_tensor", tmpf[:, 0:DC], tmpf[:, 0:DC],
                  rep_b(320, DC, 1), ALU.subtract)
                E("vector", "tensor_tensor",
                  tails_s[:, DC * (h - 1):DC * h], tmpf[:, 0:DC],
                  carry_xm[:, :], ALU.add, inc=1)
                E("vector", "tensor_tensor",
                  AP(sqb, 0, [[DC * 260, 128], [260, DC], [1, TW - 1]]),
                  AP(x2, 1, [[DC * X2W, 128], [X2W, DC], [1, TW - 1]]),
                  rep_b(0, DC, TW - 1, off=1), ALU.mult)
                E("vector", "tensor_tensor",
                  AP(sqb, 0, [[DC * 260, 128], [260, DC], [1, TW - 1]]),
                  AP(sqb, 0, [[DC * 260, 128], [260, DC], [1, TW - 1]]),
                  rep_b(320, DC, TW - 1, off=1), ALU.subtract, inc=1)
                yield
                v_out = E("vector", "tensor_tensor",
                          AP(xb, 0, [[DC * BT, 128], [BT, DC], [1, TW - 1]]),
                          AP(xb, 0, [[DC * BT, 128], [BT, DC], [1, TW - 1]]),
                          AP(sqb, 0, [[DC * 260, 128], [260, DC], [1, TW - 1]]),
                          ALU.add, inc=1)
            if h < NB - 1:
                E("vector", "tensor_copy", carry_xm[:, :],
                  AP(xb, 255, [[DC * BT, 128], [BT, DC], [1, 1]]), inc=1)
            wst = 255 if h < NB - 1 else 256
            W("sync", sV, v_out)
            v_store[h] = dma_out(
                AP(outT, 256 * h, [[T, 128], [128 * T, DC], [1, wst]]),
                AP(xb, 0, [[DC * BT, 128], [BT, DC], [1, wst]]))
            if h + 2 < NB:
                W("sync", sST, v_store[h])
                v_xload[h + 2] = dma_in(
                    AP(xbufs[(h + 2) % 2], 0,
                       [[DC * BT, 128], [BT, DC], [1, BT]]),
                    AP(xT, BT * (h + 2), [[T, 128], [128 * T, DC], [1, BT]]))
            yield

        xbufs = [x0, x1]
        v_xload = [0] * NB
        v_store = [0] * NB
        v_xload[0] = dma_in(AP(x0, 0, [[DC * BT, 128], [BT, DC], [1, BT]]),
                            AP(xT, 0, [[T, 128], [128 * T, DC], [1, BT]]))
        W("sync", sV, v_xmh)
        v_xload[1] = dma_in(AP(x1, 0, [[DC * BT, 128], [BT, DC], [1, BT]]),
                            AP(xT, BT, [[T, 128], [128 * T, DC], [1, BT]]))

        c_exp_last = c_he
        x2 = x2t
        tail_state = {"c_si": 0, "v_h2q": 0, "g_conv": 0}
        if _EAGER_WPREP:
            for _ in wprep_gen():
                pass
            tail_gen = None
            halo_part2()
        else:
            tail_gen = wprep_gen()
        def emit_head(h):
            xb = xbufs[h % 2]
            v_xn1 = phase_ln1_qkv(xb, v_xload[h])
            tap(f"xn_{h}", xn[:, :], DC * BT)
            v_q = proj(wqb, xn, xq, v_xn1, v_wcast)
            v_k = proj(wkb, xn, xk, v_xn1, v_wcast)
            v_v = proj_v(xn, v_xn1, v_wcast)
            tap(f"xq_{h}", xq[:, :], DC * BT)
            tap(f"xk_{h}", xk[:, :], DC * BT)
            heat(12)
            v_bias = phase_bound(v_q, v_k, c_exp_last)
            tap(f"bias_{h}", smallf[:, 0:1], 1, dt="f")
            return (v_q, v_k, v_v, v_bias)

        head = emit_head(0)
        for h in range(NB):
            xb = xbufs[h % 2]
            (v_q, v_k, v_v, v_bias) = head
            c_exp_last, v_ao = phase_attention(v_q, v_k, v_v, v_bias, xn,
                                               inject=tail_gen)
            if tail_gen is not None:
                for _ in tail_gen:
                    pass
                tail_gen = None
            tap(f"ao_{h}", xn[:, :], DC * BT)
            v_ao2 = proj_wo(v_ao)
            heat(20)
            tap(f"ao2_{h}", ptb[:, 0:DC * BT], DC * BT)
            v_ln3 = ln_stats(ptb, BT, BT)
            v1 = replicate(rowc(1, 0, 0, BT), 0, BT, after_v=v_ln3)
            v2 = replicate(rowc(2, 0, 0, BT), 256, BT, after_v=v_ln3)
            W("vector", sV, max(v1, v2, v_ao2))
            E("vector", "tensor_tensor",
              AP(sqb, 0, [[DC * 260, 128], [260, DC], [1, BT]]),
              AP(ptb, 0, [[2064, 128], [BT, DC], [1, BT]]),
              rep_b(0, DC, BT), ALU.mult)
            E("vector", "tensor_tensor",
              AP(sqb, 0, [[DC * 260, 128], [260, DC], [1, BT]]),
              AP(sqb, 0, [[DC * 260, 128], [260, DC], [1, BT]]),
              rep_b(256, DC, BT), ALU.subtract)
            v_xmid = E("vector", "tensor_tensor",
                       AP(xb, 0, [[DC * BT, 128], [BT, DC], [1, BT]]),
                       AP(xb, 0, [[DC * BT, 128], [BT, DC], [1, BT]]),
                       AP(sqb, 0, [[DC * 260, 128], [260, DC], [1, BT]]),
                       ALU.add, inc=1)
            tap(f"xmid_{h}", xb[:, :], DC * BT, dt="f")
            if h + 1 < NB:
                head = emit_head(h + 1)
            if h == 0 and not _EAGER_WPREP:
                halo_part2()
            heat(12)
            W("vector", sV, v_xmid)
            E("vector", "tensor_copy", xn[:, :], xb[:, :], inc=1)
            v_ln2 = ln_stats(xn, BT, BT)
            v_m2 = replicate(rowc(0, 0, 0, BT), 0, BT, after_v=v_ln2)
            W("vector", sV, v_m2)
            W("vector", sC, c_exp_last)
            E("vector", "tensor_tensor",
              AP(ps_s, 0, [[2048, 128], [256, DC], [1, BT]]),
              AP(xb, 0, [[DC * BT, 128], [BT, DC], [1, BT]]),
              rep_b(0, DC, BT), ALU.subtract, inc=1)
            E("vector", "tensor_reduce", amax1[:, 0:BT],
              AP(ps_s, 0, [[2048, 128], [1, BT], [256, DC]]),
              AX.X, ALU.max, apply_absolute_value=True, inc=1)
            v_am1 = sV.n
            for u in range(2):
                bank, bi = acq()
                W("tensor", sV, v_am1 if u == 0 else sV.n)
                g = E("tensor", "transpose", bank[:, 0:128],
                      amax1[:, 128 * u:128 * u + 128], IDENT_F, inc=1)
                W("vector", sT, g)
                E("vector", "tensor_reduce", rcol[:, 4:5], bank[:, 0:128],
                  AX.X, ALU.max, inc=1)
                v = sV.n
                rel(bi, v)
                bank, bi = acq()
                W("tensor", sV, v)
                g = E("tensor", "transpose", bank[0:1, 0:128], rcol[:, 4:5],
                      IDENT_F, inc=1)
                W("vector", sT, g)
                E("vector", "tensor_copy", rowc(0, 3, 128 * u, 128 * u + 128),
                  bank[0:1, 0:128], inc=1)
                rel(bi, sV.n)
            E("vector", "tensor_tensor", rowc(0, 3, 0, BT), rowc(0, 3, 0, BT),
              rowc(1, 0, 0, BT), ALU.mult)
            E("vector", "tensor_scalar", rowc(0, 3, 0, BT), rowc(0, 3, 0, BT),
              1e-5, None, ALU.max)
            E("vector", "reciprocal", rowc(1, 3, 0, BT), rowc(0, 3, 0, BT))
            E("vector", "tensor_scalar", rowc(1, 3, 0, BT), rowc(1, 3, 0, BT),
              127.0, None, ALU.mult)
            E("vector", "tensor_tensor", rowc(2, 3, 0, BT), rowc(1, 0, 0, BT),
              rowc(1, 3, 0, BT), ALU.mult)
            E("vector", "tensor_scalar", rowc(1, 3, 0, BT), rowc(0, 3, 0, BT),
              smallf[0:1, 4:5], 1.0 / 127.0, ALU.mult, ALU.mult, inc=1)
            v_chain = sV.n
            va = replicate(rowc(2, 3, 0, BT), 256, BT, after_v=v_chain)
            vc = replicate(rowc(1, 3, 0, BT), 640, BT, after_v=v_chain)
            W("vector", sV, max(va, vc))
            for f in range(DC):
                E("vector", "tensor_tensor", i32s[:, 0:BT],
                  ps_s[:, f * BT:f * BT + BT], reps[:, 256:256 + BT],
                  ALU.mult)
                E("vector", "tensor_copy", sqb[:, f * 260:f * 260 + BT],
                  i32s[:, 0:BT], inc=(1 if f == DC - 1 else 0))
            v_x1q = sV.n
            tap(f"x1q_{h}", sqb[:, :], DC * 260)
            W("vector", sV, hstate["v_carry"])
            E("vector", "tensor_copy",
              AP(h1b, 0, [[HC * H1W, 128], [H1W, HC], [1, 2]]),
              AP(carry_h1, 0, [[2 * HC, 128], [2, HC], [1, 2]]), inc=1)
            if h == NB - 1:
                W("vector", sV, hstate["v_h1s"])
                E("vector", "tensor_copy",
                  AP(h1b, 258, [[HC * H1W, 128], [H1W, HC], [1, 1]]),
                  AP(h1R, 0, [[HC, 128], [1, HC], [1, 1]]), inc=1)
            heat(10)
            for m0 in range(0, HC, 2):
                bank, bi = acq()
                W("tensor", sV, max(v_x1q, wprep_state["v"]))
                for j in range(2):
                    m = m0 + j
                    for k in range(DC):
                        MM(bank[:, 256 * j:256 * j + BT],
                           w1q[:, HN * k + 128 * m:HN * k + 128 * m + 128],
                           sqb[:, k * 260:k * 260 + BT],
                           (j == 0 and k == 0), k == DC - 1,
                           inc=(1 if (j == 1 and k == DC - 1) else 0))
                g = sT.n
                W("vector", sT, g)
                v = E("vector", "tensor_tensor",
                      AP(h1b, H1W * m0 + 2,
                         [[HC * H1W, 128], [H1W, 2], [1, BT]]),
                      AP(bank, 0, [[512, 128], [256, 2], [1, BT]]),
                      rep_b(640, 2, BT), ALU.mult, inc=1)
                rel(bi, v)
            E("vector", "tensor_copy",
              AP(carry_h1, 0, [[2 * HC, 128], [2, HC], [1, 2]]),
              AP(h1b, 256, [[HC * H1W, 128], [H1W, HC], [1, 2]]), inc=1)
            hstate["v_carry"] = sV.n
            v_h1ready = sV.n
            tap(f"repsF_{h}", reps[:, :], 896, dt="f")
            tap(f"smallf_{h}", smallf[:, :], 16, dt="f")
            tap(f"h1_{h}", h1b[:, :], HC * H1W)
            tail_gen = make_tail(h, xb, v_h1ready)

        if tail_gen is not None:
            for _ in tail_gen:
                pass
            tail_gen = None


        dma_out(tails_t.ap(), tails_s[:, :])
        Q["sync"].append(lambda e, v=sST.n: e.wait_ge(sST.h, v))

        @blk.sync
        def _(e):
            for f in Q["sync"]:
                f(e)

        @blk.tensor
        def _(e):
            for f in Q["tensor"]:
                f(e)

        @blk.vector
        def _(e):
            for f in Q["vector"]:
                f(e)

        @blk.scalar
        def _(e):
            for f in Q["scalar"]:
                f(e)

        @blk.gpsimd
        def _(e):
            for f in Q["gpsimd"]:
                f(e)

    _es.close()
    return nc


_NC_CACHE = {}


def _get_nc():
    if "nc" not in _NC_CACHE:
        _NC_CACHE["nc"] = build_program()
    return _NC_CACHE["nc"]


def _consts_np():
    c = np.zeros((128, 392), np.float32)
    c[:, 0:128] = np.eye(128, dtype=np.float32)
    c[:, 128] = 1.0
    c[0:64, 129] = 1.0
    c[64:128, 130] = 1.0
    sw = np.zeros((128, 128), np.float32)
    for m in range(128):
        sw[(m + 64) % 128, m] = 1.0
    c[:, 131:259] = sw
    for b in (0, 32, 64, 96):
        c[b, 260:388] = 1.0
    c[:, 388] = 1e-5
    c[:, 389] = 1e-30
    return c


def kernel(L0_attn_x, L0_attn_wq, L0_attn_wk, L0_attn_wv, L0_attn_wo,
           L0_ffn_w1, L0_ffn_conv_w, L0_ffn_conv_b, L0_ffn_w2):
    x = np.ascontiguousarray(np.asarray(L0_attn_x, dtype=np.float32))
    wq = np.asarray(L0_attn_wq, np.float32)
    wk = np.asarray(L0_attn_wk, np.float32)
    wv = np.asarray(L0_attn_wv, np.float32)
    wo = np.asarray(L0_attn_wo, np.float32)
    w1 = np.asarray(L0_ffn_w1, np.float32)
    cw = np.asarray(L0_ffn_conv_w, np.float32)
    cb = np.asarray(L0_ffn_conv_b, np.float32)
    w2 = np.asarray(L0_ffn_w2, np.float32)

    wqT = np.ascontiguousarray(wq.T)
    wkT = np.ascontiguousarray(wk.T)
    wvT = np.ascontiguousarray(wv.T)
    woT = np.ascontiguousarray(wo.T)
    w1T = np.ascontiguousarray(w1.T)
    w2T = np.ascontiguousarray(w2.T)

    consts = _consts_np()
    cwb = np.zeros((128, 128), np.float32)
    cwr = cw.reshape(HC, 128, 3)
    for c in range(HC):
        cwb[:, 3 * c:3 * c + 3] = cwr[c]
    cbr = cb.reshape(HC, 128)
    for c in range(HC):
        cwb[:, 96 + c] = cbr[c]

    in_maps = make_in_maps(x, wqT, wkT, wvT, woT, w1T, w2T, consts, cwb)
    nc = _get_nc()
    res = run_bass_kernel_spmd(nc, in_maps, core_ids=list(range(8)))
    return _assemble(res)


def make_in_maps(x, wqT, wkT, wvT, woT, w1T, w2T, consts, cwb):
    in_maps = []
    for core in range(8):
        b, half = core // 2, core % 2
        xs = np.ascontiguousarray(x[b, T * half:T * half + T].T)
        if half == 0:
            hb = x[b, T:T + BT]
            fl = (0.0, 1.0)
        else:
            hb = np.roll(x[b, T - BT:T], 1, axis=0)
            fl = (1.0, 0.0)
        xTh = np.ascontiguousarray(hb.T)
        flags = np.zeros((128, 2), np.float32)
        flags[:, 0] = fl[0]
        flags[:, 1] = fl[1]
        in_maps.append({
            "xT": xs, "xTh": xTh, "flags": flags,
            "wqT": wqT, "wkT": wkT, "wvT": wvT, "woT": woT,
            "w1T": w1T, "w2T": w2T, "consts": consts, "cwb": cwb,
        })
    return in_maps


def _assemble(res):
    out = np.empty((4, 4096, D), np.float32)
    for core in range(8):
        b, half = core // 2, core % 2
        o = np.asarray(res.results[core]["outT"])
        tails = np.asarray(res.results[core]["out_tails"])
        ob = o.T.copy()
        for h in range(NB - 1):
            ob[256 * h + 255, :] = tails[:, DC * h:DC * (h + 1)].T.reshape(-1)
        out[b, T * half:T * half + T] = ob
    return out


# version 29
# speedup vs baseline: 1.0416x; 1.0416x over previous
import numpy as np
import ml_dtypes
import concourse.bass as bass
import concourse.mybir as mybir
from concourse.bass_utils import run_bass_kernel_spmd

F32 = mybir.dt.float32
BF16 = mybir.dt.bfloat16
_EAGER_WPREP = False
FP8 = mybir.dt.float8e4
I32 = mybir.dt.int32
ALU = mybir.AluOpType
ACTF = mybir.ActivationFunctionType
AX = mybir.AxisListType
AP = bass.AP

T = 2048
NB = 8
BT = 256
D = 1024
DC = 8
HN = 4096
HC = 32
VW = 66
SCALE = 0.125
H1W = 259
X2W = 257
YW = 130


_TAPS = ()


class Sem:
    __slots__ = ("h", "n")

    def __init__(self, h):
        self.h = h
        self.n = 0


def _win(h):
    if h == 0:
        return 0, 255
    if h == NB - 1:
        return 256 * h - 1, 257
    return 256 * h - 1, 256


def build_program(taps=()):
    global _TAPS
    _TAPS = tuple(taps)
    nc = bass.Bass("TRN2", target_bir_lowering=False,
                   dynamic_dma_scratch_size=4096)

    xT = nc.dram_tensor("xT", [D, T], F32, kind="ExternalInput")
    xTh = nc.dram_tensor("xTh", [D, BT], F32, kind="ExternalInput")
    flags_t = nc.dram_tensor("flags", [128, 2], F32, kind="ExternalInput")
    wqT = nc.dram_tensor("wqT", [D, D], F32, kind="ExternalInput")
    wkT = nc.dram_tensor("wkT", [D, D], F32, kind="ExternalInput")
    wvT = nc.dram_tensor("wvT", [D, D], F32, kind="ExternalInput")
    woT = nc.dram_tensor("woT", [D, D], F32, kind="ExternalInput")
    w1Tt = nc.dram_tensor("w1T", [D, HN], F32, kind="ExternalInput")
    w2Tt = nc.dram_tensor("w2T", [HN, D], F32, kind="ExternalInput")
    consts_t = nc.dram_tensor("consts", [128, 392], F32, kind="ExternalInput")
    cwb_t = nc.dram_tensor("cwb", [128, 128], F32, kind="ExternalInput")
    outT = nc.dram_tensor("outT", [D, T], F32, kind="ExternalOutput")
    tails_t = nc.dram_tensor("out_tails", [128, DC * NB], F32,
                             kind="ExternalOutput")
    dbgf_t = nc.dram_tensor("dbgf", [128, 32768], F32, kind="ExternalOutput")
    dbgb_t = nc.dram_tensor("dbgb", [128, 32768], BF16, kind="ExternalOutput")

    Q = {k: [] for k in ("sync", "tensor", "vector", "scalar", "gpsimd")}

    from contextlib import ExitStack
    _es = ExitStack()
    x0 = _es.enter_context(nc.sbuf_tensor("x0", [128, DC * BT], F32))
    x1 = _es.enter_context(nc.sbuf_tensor("x1", [128, DC * BT], F32))
    xn = _es.enter_context(nc.sbuf_tensor("xn", [128, DC * BT], BF16))
    sqb = _es.enter_context(nc.sbuf_tensor("sqb", [128, DC * 260], BF16))
    rowbank = _es.enter_context(nc.sbuf_tensor("rowbank", [128, 1300], F32))
    reps = _es.enter_context(nc.sbuf_tensor("reps", [128, 896], F32))
    xq = _es.enter_context(nc.sbuf_tensor("xq", [128, DC * BT], BF16))
    xk = _es.enter_context(nc.sbuf_tensor("xk", [128, DC * BT], BF16))
    qrhs = _es.enter_context(nc.sbuf_tensor("qrhs", [128, 2 * 3 * BT], BF16))
    v65 = _es.enter_context(nc.sbuf_tensor("v65", [128, 2 * 16 * VW], BF16))
    ptb = _es.enter_context(nc.sbuf_tensor("ptb", [128, 2064], BF16))
    stage = _es.enter_context(nc.sbuf_tensor("stage", [128, 256], BF16))
    rcol = _es.enter_context(nc.sbuf_tensor("rcol", [128, 8], F32))
    smallf = _es.enter_context(nc.sbuf_tensor("smallf", [128, 16], F32))
    wqb = _es.enter_context(nc.sbuf_tensor("wqb", [128, DC * D], BF16))
    wkb = _es.enter_context(nc.sbuf_tensor("wkb", [128, DC * D], BF16))
    wvb = _es.enter_context(nc.sbuf_tensor("wvb", [128, DC * D], BF16))
    wos0 = _es.enter_context(nc.sbuf_tensor("wos0", [128, D], BF16))
    wos1 = _es.enter_context(nc.sbuf_tensor("wos1", [128, D], BF16))
    w1q = _es.enter_context(nc.sbuf_tensor("w1q", [128, DC * HN], FP8))
    w2q = _es.enter_context(nc.sbuf_tensor("w2q", [128, HC * D], FP8))
    constf = _es.enter_context(nc.sbuf_tensor("constf", [128, 392], F32))
    cwbf = _es.enter_context(nc.sbuf_tensor("cwbf", [128, 128], F32))
    flagf = _es.enter_context(nc.sbuf_tensor("flagf", [128, 2], F32))
    identb = _es.enter_context(nc.sbuf_tensor("identb", [128, 128], BF16))
    swapb = _es.enter_context(nc.sbuf_tensor("swapb", [128, 128], BF16))
    onesb = _es.enter_context(nc.sbuf_tensor("onesb", [128, 4], BF16))
    h1L = _es.enter_context(nc.sbuf_tensor("h1L", [128, HC], BF16))
    h1R = _es.enter_context(nc.sbuf_tensor("h1R", [128, HC], BF16))
    qc = _es.enter_context(nc.sbuf_tensor("qc", [128, 16], BF16))
    hcol = _es.enter_context(nc.sbuf_tensor("hcol", [128, 32], BF16))
    in2 = _es.enter_context(nc.sbuf_tensor("in2", [8, 128], BF16))
    onorm = _es.enter_context(nc.sbuf_tensor("onorm", [16, 64], BF16))
    tails_s = _es.enter_context(nc.sbuf_tensor("tails", [128, DC * NB], F32))
    carry_xm = _es.enter_context(nc.sbuf_tensor("carry_xm", [128, DC], F32))
    carry_h1 = _es.enter_context(nc.sbuf_tensor("carry_h1", [128, 2 * HC], BF16))
    h1b = _es.enter_context(nc.sbuf_tensor("h1b", [128, HC * H1W], BF16))
    yb = _es.enter_context(nc.sbuf_tensor("yb", [128, HC * YW], BF16))
    h2f = _es.enter_context(nc.sbuf_tensor("h2f", [128, HC * 66], F32))
    h2qb = _es.enter_context(nc.sbuf_tensor("h2qb", [128, HC * 66], BF16))
    i32s = _es.enter_context(nc.sbuf_tensor("i32s", [128, 264], I32))
    tmpf = _es.enter_context(nc.sbuf_tensor("tmpf", [128, 264], F32))
    x2t = _es.enter_context(nc.sbuf_tensor("x2t", [128, DC * X2W], BF16))
    amax1 = tmpf
    wacc = _es.enter_context(nc.sbuf_tensor("wacc", [128, 32], F32))
    ps_s = _es.enter_context(nc.psum_tensor("ps_s", [128, 2048], F32))
    ps_o = _es.enter_context(nc.psum_tensor("ps_o", [128, 512], F32))
    ps_a = _es.enter_context(nc.psum_tensor("ps_a", [128, 512], F32))
    ps_b = _es.enter_context(nc.psum_tensor("ps_b", [128, 512], F32))
    ps_t = _es.enter_context(nc.psum_tensor("ps_t", [128, 1024], BF16))
    h_ld = _es.enter_context(nc.semaphore("s_ld"))
    h_st = _es.enter_context(nc.semaphore("s_st"))
    h_t = _es.enter_context(nc.semaphore("s_t"))
    h_v = _es.enter_context(nc.semaphore("s_v"))
    h_c = _es.enter_context(nc.semaphore("s_c"))
    h_g = _es.enter_context(nc.semaphore("s_g"))
    h_w0 = _es.enter_context(nc.semaphore("s_w0"))
    h_w1 = _es.enter_context(nc.semaphore("s_w1"))
    with nc.Block() as blk:

        sLD, sST = Sem(h_ld), Sem(h_st)
        sW = [Sem(h_w0), Sem(h_w1)]
        sT, sV, sC, sG = Sem(h_t), Sem(h_v), Sem(h_c), Sem(h_g)
        ESEM = {"tensor": sT, "vector": sV, "scalar": sC, "gpsimd": sG}

        def _fd(ap):
            try:
                sh = ap.shape
            except Exception:
                return 1 << 20
            n = 1
            for s in sh[1:]:
                n *= int(s)
            return n

        def E(eng, meth, *a, inc=0, bar=None, **kw):
            sem = ESEM.get(eng)
            if bar is None:
                bar = (eng in ("vector", "scalar") and meth != "memset"
                       and len(a) > 0 and _fd(a[0]) < 64)
            if bar and not inc:
                inc = 1
            if inc:
                sem.n += inc

            def f(e, meth=meth, a=a, kw=kw, inc=inc, sem=sem):
                ins = getattr(e, meth)(*a, **kw)
                if inc:
                    ins.then_inc(sem.h, inc)
            Q[eng].append(f)
            if bar:
                Q[eng].append(lambda e, s=sem.h, v=sem.n: e.wait_ge(s, v))
            return sem.n if sem else 0

        def W(eng, sem, val):
            assert val <= sem.n, f"wait-on-future {eng}: {val}>{sem.n}"
            if val > 0:
                Q[eng].append(lambda e, s=sem.h, v=val: e.wait_ge(s, v))

        def MM(out, lhsT, rhs, start, stop, inc=0, tp=None):
            return E("tensor", "matmul", out, lhsT, rhs, start=start,
                     stop=stop, skip_group_check=True, tile_position=tp,
                     inc=inc)

        def dma_in(dst, src, sem=None, chain=True):
            s = sLD if sem is None else sem
            prev = s.n
            s.n += 16

            def f(e, dst=dst, src=src, prev=prev, s=s, chain=chain):
                if chain:
                    e.wait_ge(s.h, prev)
                e.dma_start(dst, src).then_inc(s.h, 16)
            Q["sync"].append(f)
            return s.n

        def dma_out(dst, src):
            prev = sST.n
            sST.n += 16

            def f(e, dst=dst, src=src, prev=prev):
                e.wait_ge(sST.h, prev)
                e.dma_start(dst, src).then_inc(sST.h, 16)
            Q["sync"].append(f)
            return sST.n

        tapoff = {"f": 0, "b": 0}
        TAPMAP = {}

        def tap(name, ap, width, dt="b"):
            if name not in _TAPS:
                return
            W("sync", sV, sV.n)
            W("sync", sT, sT.n)
            dst_t = dbgf_t if dt == "f" else dbgb_t
            o = tapoff[dt]
            sST.n += 16

            def f(e, dst=AP(dst_t, o, [[32768, 128], [1, width]]), src=ap):
                e.dma_start(dst, src,
                            allow_slow_non_contiguous=True).then_inc(sST.h, 16)
            Q["sync"].append(f)
            TAPMAP[name] = (dt, o, width)
            tapoff[dt] = o + width
            W("vector", sST, sST.n)
            W("scalar", sST, sST.n)
            W("tensor", sST, sST.n)
            return
            TAPMAP[name] = (dt, o, width)
            tapoff[dt] = o + width
        nc._tapmap = TAPMAP

        misc = {"rot": 0, "reader": [0, 0]}

        def acq():
            i = misc["rot"] % 2
            misc["rot"] += 1
            W("tensor", sV, misc["reader"][i])
            return [ps_a, ps_b][i], i

        def rel(i, val):
            misc["reader"][i] = val

        def heat(n):
            bank, bi = acq()
            for _ in range(n):
                MM(bank[:, 0:512], wqb[:, 0:128], wkb[:, 0:512], True, True)
            rel(bi, misc["reader"][bi])

        _RSLOT = {(0, 0): 0, (1, 0): 260, (2, 0): 520, (3, 0): 780,
                  (0, 1): 1040, (0, 2): 1056, (0, 3): 780,
                  (1, 3): 1040, (2, 3): 0, (3, 3): 520,
                  (1, 1): 1040, (2, 1): 1170}

        def rowc(b, j, lo, hi):
            off = _RSLOT[(b, j)]
            return rowbank[0:1, off + lo:off + hi]

        v_fl = 0
        dma_in(constf[:, :], consts_t.ap())
        dma_in(cwbf[:, :], cwb_t.ap())
        v_fl = dma_in(flagf[:, :], flags_t.ap())
        W("vector", sLD, v_fl)
        E("vector", "tensor_copy", identb[:, :], constf[:, 0:128])
        E("vector", "tensor_copy", swapb[:, :], constf[:, 131:259])
        E("vector", "tensor_copy", onesb[:, 0:3], constf[:, 128:131])
        E("vector", "memset",
          AP(v65, 64, [[2 * 16 * VW, 128], [VW, 32], [1, 1]]), 1.0, inc=1)
        v_base = sV.n

        ONES_F = constf[:, 128:129]
        ONES_ROW_F = constf[0:1, 260:388]
        IDENT_F = constf[:, 0:128]
        ONES_B = onesb[:, 0:1]
        HALF0_B = onesb[:, 1:2]
        HALF1_B = onesb[:, 2:3]

        last_stage_user = [v_base]

        wstage = h2f
        _ws_state = {"n": 0, "user": [0, 0]}

        def load_piece(src_t, elem_off, row_stride=1024):
            wi = _ws_state["n"] % 2
            _ws_state["n"] += 1
            W("sync", sV, _ws_state["user"][wi])
            v = dma_in(AP(wstage, 1024 * wi, [[2112, 128], [1, 1024]]),
                       AP(src_t, elem_off, [[row_stride, 128], [1, 1024]]),
                       sem=sW[wi], chain=False)
            return v, wi

        def stage_slice(wi, lo, hi):
            return wstage[:, 1024 * wi + lo:1024 * wi + hi]

        for src, dst in ((wqT, wqb), (wkT, wkb), (wvT, wvb)):
            for f in range(DC):
                v, wi = load_piece(src, f * 128 * D)
                W("vector", sW[wi], v)
                _ws_state["user"][wi] = E("vector", "tensor_copy",
                                          dst[:, f * D:(f + 1) * D],
                                          stage_slice(wi, 0, 1024), inc=1)
        v_wcast = max(_ws_state["user"])

        def ternary_prep(srcT, nrow_chunks, ncols, dstq, slot):
            pieces = []
            for k in range(nrow_chunks):
                for cp in range(ncols // 1024):
                    pieces.append((k, cp))
            for pi, (k, cp) in enumerate(pieces):
                v, wi = load_piece(srcT, k * 128 * ncols + cp * 1024, ncols)
                W("vector", sW[wi], v)
                _ws_state["user"][wi] = E("vector", "tensor_reduce",
                                          wacc[:, pi:pi + 1],
                                          stage_slice(wi, 0, 1024),
                                          AX.X, ALU.add,
                                          apply_absolute_value=True, inc=1)
                if pi % 4 == 3:
                    yield
            npieces = len(pieces)
            bank, bi = acq()
            W("tensor", sV, max(_ws_state["user"]))
            tap(f"wacc_{slot}", wacc[:, :], 32, dt="f")
            g = MM(bank[0:1, 0:npieces], ONES_F, wacc[:, 0:npieces],
                   True, True, inc=1)
            W("vector", sT, g)
            E("vector", "tensor_reduce", rowc(0, 0, 0, 1),
              bank[0:1, 0:npieces], AX.X, ALU.add)
            E("vector", "tensor_scalar", rowc(0, 0, 1, 2), rowc(0, 0, 0, 1),
              1.0 / (HN * D), 1e-5, ALU.mult, ALU.max)
            v = E("vector", "reciprocal", rowc(0, 0, 2, 3), rowc(0, 0, 1, 2),
                  inc=1)
            rel(bi, v)
            tap(f"wsrow_{slot}", rowbank[:, 0:4], 4, dt="f")
            bank, bi = acq()
            W("tensor", sV, v)
            MM(bank[:, 0:1], ONES_ROW_F, rowc(0, 0, 1, 2), True, True)
            g = MM(bank[:, 1:2], ONES_ROW_F, rowc(0, 0, 2, 3), False, True,
                   inc=1)
            W("vector", sT, g)
            v = E("vector", "tensor_copy", smallf[:, slot:slot + 2],
                  bank[:, 0:2], inc=1)
            rel(bi, v)
            for (k, cp) in pieces:
                v, wi = load_piece(srcT, k * 128 * ncols + cp * 1024, ncols)
                W("vector", sW[wi], v)
                base = k * ncols + cp * 1024
                for s4 in range(4):
                    E("vector", "tensor_scalar", i32s[:, 0:256],
                      stage_slice(wi, 256 * s4, 256 * s4 + 256),
                      smallf[:, slot + 1:slot + 2], 1, ALU.mult, ALU.min)
                    E("vector", "tensor_scalar", i32s[:, 0:256],
                      i32s[:, 0:256], -1, None, ALU.max)
                    E("vector", "tensor_copy",
                      dstq[:, base + 256 * s4:base + 256 * s4 + 256],
                      i32s[:, 0:256],
                      inc=(1 if s4 == 3 else 0))
                _ws_state["user"][wi] = sV.n
                yield
            return sV.n

        wprep_state = {"v": 0}

        def wprep_gen():
            for _ in ternary_prep(w1Tt, DC, HN, w1q, 4):
                yield
            for _ in ternary_prep(w2Tt, HC, D, w2q, 6):
                yield
            wprep_state["v"] = sV.n
            yield


        def ln_stats(src, stride, tok):
            E("vector", "tensor_tensor",
              AP(sqb, 0, [[DC * 260, 128], [260, DC], [1, tok]]),
              AP(src, 0, [[src.shape[1], 128], [stride, DC], [1, tok]]),
              AP(src, 0, [[src.shape[1], 128], [stride, DC], [1, tok]]),
              ALU.mult, inc=1)
            v_sq = sV.n
            bankA, biA = acq()
            bankB, biB = acq()
            W("tensor", sV, v_sq)
            for f in range(DC):
                MM(bankA[0:1, 0:tok], ONES_B,
                   src[:, f * stride:f * stride + tok], f == 0, f == DC - 1)
            for f in range(DC):
                MM(bankB[0:1, 0:tok], ONES_B, sqb[:, f * 260:f * 260 + tok],
                   f == 0, f == DC - 1, inc=(1 if f == DC - 1 else 0))
            g = sT.n
            W("vector", sT, g)
            E("vector", "tensor_scalar", rowc(0, 0, 0, tok), bankA[0:1, 0:tok],
              1.0 / D, None, ALU.mult)
            E("vector", "tensor_scalar", rowc(3, 0, 0, tok), bankB[0:1, 0:tok],
              1.0 / D, None, ALU.mult)
            E("vector", "tensor_tensor", rowc(1, 0, 0, tok), rowc(0, 0, 0, tok),
              rowc(0, 0, 0, tok), ALU.mult)
            v = E("vector", "tensor_tensor", rowc(3, 0, 0, tok),
                  rowc(3, 0, 0, tok), rowc(1, 0, 0, tok), ALU.subtract, inc=1)
            rel(biA, v)
            rel(biB, v)
            W("scalar", sV, v)
            E("scalar", "activation", rowc(3, 0, 0, tok), rowc(3, 0, 0, tok),
              ACTF.Ln, bias=constf[0:1, 388:389], scale=1.0)
            c = E("scalar", "activation", rowc(1, 0, 0, tok),
                  rowc(3, 0, 0, tok), ACTF.Exp, bias=0.0, scale=-0.5, inc=1)
            W("vector", sC, c)
            return E("vector", "tensor_tensor", rowc(2, 0, 0, tok),
                     rowc(0, 0, 0, tok), rowc(1, 0, 0, tok), ALU.mult, inc=1)

        def replicate(row_ap, col, tok, after_v=0):
            assert tok <= 512
            bank, bi = acq()
            W("tensor", sV, after_v)
            g = MM(bank[:, 0:tok], ONES_ROW_F, row_ap, True, True, inc=1)
            W("vector", sT, g)
            v = E("vector", "tensor_copy", reps[:, col:col + tok],
                  bank[:, 0:tok], inc=1)
            rel(bi, v)
            return v

        def rep_b(col, n, w, off=0):
            return AP(reps, col + off, [[896, 128], [0, n], [1, w]])

        def proj(wb, src, dst, v_src, w_ready):
            out = 0
            for m0 in range(0, DC, 2):
                bank, bi = acq()
                W("tensor", sV, max(v_src, w_ready))
                for j in range(2):
                    m = m0 + j
                    for k in range(DC):
                        MM(bank[:, 256 * j:256 * j + BT],
                           wb[:, D * k + 128 * m:D * k + 128 * m + 128],
                           src[:, k * BT:k * BT + BT],
                           (j == 0 and k == 0), k == DC - 1,
                           inc=(1 if (j == 1 and k == DC - 1) else 0))
                g = sT.n
                W("vector", sT, g)
                out = E("vector", "tensor_copy",
                        dst[:, m0 * BT:m0 * BT + 512],
                        bank[:, 0:512], inc=1)
                rel(bi, out)
            return out

        def proj_v(src, v_src, w_ready):
            out = 0
            for tt in range(2):
                for half in range(2):
                    bank, bi = acq()
                    W("tensor", sV, max(v_src, w_ready))
                    for k in range(DC):
                        MM(bank[:, 0:512],
                           src[:, k * BT + 128 * tt:k * BT + 128 * tt + 128],
                           wvb[:, D * k + 512 * half:D * k + 512 * half + 512],
                           k == 0, k == DC - 1, inc=(1 if k == DC - 1 else 0))
                    g = sT.n
                    W("vector", sT, g)
                    out = E("vector", "tensor_copy",
                            AP(v65, (16 * VW) * tt + (8 * half) * VW,
                               [[2 * 16 * VW, 128], [VW, 8], [1, 64]]),
                            AP(bank, 0, [[512, 128], [64, 8], [1, 64]]),
                            inc=1)
                    rel(bi, out)
            return out

        wo_state = {"cast": [0, 0], "user": [0, 0], "cast_last": 0}
        wosf = h2f

        def proj_wo(v_src):
            out = 0
            for m in range(DC):
                wi = m % 2
                wos = [wos0, wos1][wi]
                W("sync", sV, max(wo_state["cast_last"],
                                  _ws_state["user"][1]))
                v = dma_in(AP(wosf, 1024, [[2112, 128], [128, DC], [1, 128]]),
                           AP(woT, 128 * m, [[D, 128], [128 * D, DC], [1, 128]]))
                W("vector", sLD, v)
                W("vector", sT, wo_state["user"][wi])
                wo_state["cast"][wi] = E("vector", "tensor_copy", wos[:, :],
                                         wosf[:, 1024:2048], inc=1)
                wo_state["cast_last"] = wo_state["cast"][wi]
                _ws_state["user"][1] = wo_state["cast"][wi]
                bank, bi = acq()
                W("tensor", sV, max(v_src, wo_state["cast"][wi]))
                for k in range(DC):
                    MM(bank[:, 0:BT], wos[:, 128 * k:128 * k + 128],
                       xn[:, k * BT:k * BT + BT],
                       k == 0, k == DC - 1, inc=(1 if k == DC - 1 else 0))
                g = sT.n
                wo_state["user"][wi] = g
                W("vector", sT, g)
                out = E("vector", "tensor_copy", ptb[:, m * BT:m * BT + BT],
                        bank[:, 0:BT], inc=1)
                rel(bi, out)
            return out

        def phase_ln1_qkv(xbuf, v_x):
            W("vector", sLD, v_x)
            E("vector", "tensor_copy", xn[:, :], xbuf[:, :], inc=1)
            v_ln = ln_stats(xn, BT, BT)
            v1 = replicate(rowc(1, 0, 0, BT), 0, BT, after_v=v_ln)
            v2 = replicate(rowc(2, 0, 0, BT), 256, BT, after_v=v_ln)
            W("vector", sV, max(v1, v2))
            E("vector", "tensor_tensor",
              AP(sqb, 0, [[DC * 260, 128], [260, DC], [1, BT]]),
              AP(xbuf, 0, [[DC * BT, 128], [BT, DC], [1, BT]]),
              rep_b(0, DC, BT), ALU.mult)
            return E("vector", "tensor_tensor",
                     AP(xn, 0, [[DC * BT, 128], [BT, DC], [1, BT]]),
                     AP(sqb, 0, [[DC * 260, 128], [260, DC], [1, BT]]),
                     rep_b(256, DC, BT), ALU.subtract, inc=1)

        def phase_bound(v_q, v_k, c_prev_exp):
            for (xt, vv, col) in ((xq, v_q, 0), (xk, v_k, 1)):
                W("vector", sV, vv)
                E("vector", "tensor_tensor",
                  AP(sqb, 0, [[DC * 260, 128], [260, DC], [1, BT]]),
                  AP(xt, 0, [[DC * BT, 128], [BT, DC], [1, BT]]),
                  AP(xt, 0, [[DC * BT, 128], [BT, DC], [1, BT]]),
                  ALU.mult, inc=1)
                v_sq = sV.n
                v_red = 0
                for rd in range(2):
                    W("tensor", sV, max(v_sq, v_red))
                    W("tensor", sC, c_prev_exp)
                    for sl in range(8):
                        f, hf = (8 * rd + sl) // 2, (8 * rd + sl) % 2
                        MM(ps_s[0:1, sl * 256:sl * 256 + BT],
                           HALF0_B if hf == 0 else HALF1_B,
                           sqb[:, f * 260:f * 260 + BT],
                           True, True, inc=(1 if sl == 7 else 0))
                    g = sT.n
                    W("vector", sT, g)
                    v_red = E("vector", "tensor_reduce",
                              rowc(0, 1, 4 + rd, 5 + rd),
                              ps_s[0:1, 0:2048], AX.X, ALU.max, inc=1)
                E("vector", "tensor_tensor", rowc(0, 1, col, col + 1),
                  rowc(0, 1, 4, 5), rowc(0, 1, 5, 6), ALU.max, inc=1)
            E("vector", "tensor_tensor", rowc(0, 1, 2, 3), rowc(0, 1, 0, 1),
              rowc(0, 1, 1, 2), ALU.mult, inc=1)
            v_bb = sV.n
            W("scalar", sV, v_bb)
            E("scalar", "activation", rowc(0, 1, 6, 7), rowc(0, 1, 2, 3),
              ACTF.Ln, bias=constf[0:1, 389:390], scale=1.0)
            c = E("scalar", "activation", rowc(0, 1, 7, 8), rowc(0, 1, 6, 7),
                  ACTF.Exp, bias=0.0, scale=0.5, inc=1)
            W("vector", sC, c)
            E("vector", "tensor_scalar", rowc(0, 1, 8, 9), rowc(0, 1, 7, 8),
              -SCALE, -1.0, ALU.mult, ALU.add, inc=1)
            v_brow = sV.n
            bank, bi = acq()
            W("tensor", sV, v_brow)
            g = MM(bank[:, 0:1], ONES_ROW_F, rowc(0, 1, 8, 9), True, True,
                   inc=1)
            W("vector", sT, g)
            v = E("vector", "tensor_copy", smallf[:, 0:1], bank[:, 0:1],
                  inc=1)
            rel(bi, v)
            return v

        att = {"pt_user": 0, "ao_evac": 0,
               "exp_half": [0, 0], "pt_half": [0, 0]}

        def phase_attention(v_q, v_k, v_v, v_bias, dst_ao, inject=None):
            c_exp = 0
            v_round = 0
            exp_half = att["exp_half"]
            pt_half = att["pt_half"]

            def emit_S(r, p, v_xqs, qb=0):
                hp = p % 2
                W("tensor", sC, exp_half[hp])
                for jj in range(2):
                    jt = 2 * p + jj
                    cj, sh = jt // 2, jt % 2
                    pe = cj % 2
                    lhs = xk[64 * pe:64 * pe + 64,
                             (cj // 2) * BT + 128 * sh:
                             (cj // 2) * BT + 128 * sh + 128]
                    rhs = qrhs[64 * pe:64 * pe + 64,
                               qb + 256 * pe:qb + 256 * pe + 512]
                    MM(ps_s[:, 1024 * hp + 512 * jj:
                            1024 * hp + 512 * jj + 512],
                       lhs, rhs, True, True,
                       inc=(1 if jj == 1 else 0))
                return sT.n

            def emit_exp(p, g_s):
                nonlocal c_exp
                hp = p % 2
                W("scalar", sT, g_s)
                W("scalar", sT, pt_half[hp])
                c_exp = E("scalar", "activation",
                          ptb[:, 1024 * hp:1024 * hp + 1024],
                          ps_s[:, 1024 * hp:1024 * hp + 1024], ACTF.Exp,
                          bias=smallf[:, 0:1], scale=SCALE, inc=1)
                exp_half[hp] = c_exp
                return c_exp

            def emit_PV(p, c_e, first):
                hp = p % 2
                W("tensor", sC, c_e)
                for jj in range(2):
                    jt = 2 * p + jj
                    cj, sh = jt // 2, jt % 2
                    vsl = v65[:, (16 * VW) * sh + VW * cj:
                              (16 * VW) * sh + VW * cj + 65]
                    for s in range(4):
                        MM(ps_o[:, 128 * s:128 * s + 65],
                           ptb[:, 1024 * hp + 512 * jj + 128 * s:
                               1024 * hp + 512 * jj + 128 * s + 128],
                           vsl, first, True,
                           inc=(1 if (jj == 1 and s == 3) else 0))
                        first = False
                pt_half[hp] = sT.n
                return sT.n

            def prep_qrhs(r):
                qb = 768 * (r % 2)
                bank, bi = acq()
                W("tensor", sV, v_q)
                W("tensor", sT, att["pt_user"])
                g = MM(bank[:, 0:BT], swapb[:, :], xq[:, r * BT:r * BT + BT],
                       True, True, inc=1)
                W("vector", sT, g)
                E("vector", "tensor_copy", qrhs[:, qb + 256:qb + 512],
                  bank[:, 0:BT])
                E("vector", "tensor_copy", qrhs[:, qb:qb + 256],
                  xq[:, r * BT:r * BT + BT])
                v = E("vector", "tensor_copy", qrhs[:, qb + 512:qb + 768],
                      xq[:, r * BT:r * BT + BT], inc=1)
                rel(bi, v)
                return v

            def epilogue(r):
                nonlocal v_round
                W("vector", sT, att["pt_user"])
                for s in range(4):
                    E("vector", "reciprocal", rcol[:, s:s + 1],
                      ps_o[:, 128 * s + 64:128 * s + 65])
                for s in range(4):
                    a, u = s // 2, s % 2
                    E("vector", "tensor_scalar",
                      stage[:, 128 * u + 64 * a:128 * u + 64 * a + 64],
                      ps_o[:, 128 * s:128 * s + 64],
                      rcol[:, s:s + 1], None, ALU.mult,
                      inc=(1 if s == 3 else 0))
                v_round = sV.n
                W("tensor", sV, v_round)
                W("tensor", sV, att["ao_evac"] if att["ao_evac"] else v_base)
                for u in range(2):
                    E("tensor", "transpose", ps_t[:, 128 * u:128 * u + 128],
                      stage[:, 128 * u:128 * u + 128], identb[:, :],
                      inc=(1 if u == 1 else 0))
                g_tr = sT.n
                W("vector", sT, g_tr)
                for u in range(2):
                    E("vector", "tensor_copy",
                      dst_ao[:, r * BT + 128 * u:r * BT + 128 * u + 128],
                      ps_t[:, 128 * u:128 * u + 128],
                      inc=(1 if u == 1 else 0))
                att["ao_evac"] = sV.n

            v_xqs_by = {}
            v_xqs_by[0] = prep_qrhs(0)
            for r in range(NB):
                if r + 1 < NB:
                    v_xqs_by[r + 1] = prep_qrhs(r + 1)
                v_xqs = v_xqs_by[r]
                qb = 768 * (r % 2)
                W("scalar", sV, v_bias)
                if v_round:
                    W("tensor", sV, v_round)
                W("tensor", sV, max(v_xqs, v_k, v_bias, v_v))
                gs = {}
                ce = {}
                for p in range(16):
                    gs[p] = emit_S(r, p, v_xqs, qb)
                    if p >= 1:
                        ce[p - 1] = emit_exp(p - 1, gs[p - 1])
                        emit_PV(p - 1, ce[p - 1], first=(p - 1 == 0))
                    if inject is not None:
                        next(inject, None)
                ce[15] = emit_exp(15, gs[15])
                emit_PV(15, ce[15], first=False)
                att["pt_user"] = sT.n
                epilogue(r)
                if inject is not None:
                    next(inject, None)
            return c_exp, att["ao_evac"]

        v_xh = dma_in(AP(x1, 0, [[DC * BT, 128], [BT, DC], [1, BT]]),
                      AP(xTh, 0, [[BT, 128], [128 * BT, DC], [1, BT]]))
        v_xn_h = phase_ln1_qkv(x1, v_xh)
        v_qh = proj(wqb, xn, xq, v_xn_h, v_wcast)
        v_kh = proj(wkb, xn, xk, v_xn_h, v_wcast)
        v_vh = proj_v(xn, v_xn_h, max(v_wcast, v_base))
        v_bias_h = phase_bound(v_qh, v_kh, 0)

        W("vector", sV, v_qh)
        E("vector", "tensor_copy",
          AP(qc, 0, [[16, 128], [1, 8], [1, 1]]),
          AP(xq, 0, [[DC * BT, 128], [BT, 8], [1, 1]]), inc=1)
        v_qc = sV.n
        bank, bi = acq()
        W("tensor", sV, v_qc)
        g = MM(bank[:, 0:8], swapb[:, :], qc[:, 0:8], True, True, inc=1)
        W("vector", sT, g)
        v_qcs = E("vector", "tensor_copy", qc[:, 8:16], bank[:, 0:8], inc=1)
        rel(bi, v_qcs)

        W("tensor", sV, max(v_qcs, v_kh, v_bias_h))
        for jt in range(32):
            cj, sh = jt // 2, jt % 2
            base = 64 * (cj % 2)
            lhs = xk[base:base + 64,
                     (cj // 2) * BT + 128 * sh:(cj // 2) * BT + 128 * sh + 128]
            for ci in range(16):
                f = ci // 2
                col = f if (ci % 2) == (cj % 2) else 8 + f
                MM(ps_s[:, 16 * jt + ci:16 * jt + ci + 1],
                   lhs, qc[base:base + 64, col:col + 1],
                   (jt == 0 and ci == 0), True,
                   inc=(1 if (jt == 31 and ci == 15) else 0))
        g_sh = sT.n
        W("scalar", sT, g_sh)
        W("scalar", sV, v_bias_h)
        c_he = E("scalar", "activation", ptb[:, 0:512], ps_s[:, 0:512],
                 ACTF.Exp, bias=smallf[:, 0:1], scale=SCALE, inc=1)
        W("tensor", sC, c_he)
        W("tensor", sV, v_vh)
        for jt in range(32):
            cj, sh = jt // 2, jt % 2
            vsl = v65[:, (16 * VW) * sh + VW * cj:(16 * VW) * sh + VW * cj + 65]
            MM(ps_o[0:16, 0:65], ptb[:, 16 * jt:16 * jt + 16], vsl,
               jt == 0, jt == 31, inc=(1 if jt == 31 else 0))
        g_pv = sT.n
        W("vector", sT, g_pv)
        E("vector", "reciprocal", rcol[0:16, 0:1], ps_o[0:16, 64:65])
        v_on = E("vector", "tensor_scalar", onorm[0:16, 0:64],
                 ps_o[0:16, 0:64], rcol[0:16, 0:1], None, ALU.mult, inc=1)
        W("sync", sV, v_on)
        v_in2 = dma_in(AP(in2, 0, [[128, 8], [64, 2], [1, 64]]),
                       AP(onorm, 0, [[64, 16], [1, 64]]))
        W("tensor", sLD, v_in2)
        g_ao = E("tensor", "transpose", ps_t[:, 0:8], in2[0:8, :],
                 identb[0:8, 0:8], inc=1)
        W("vector", sT, g_ao)
        v_aoh = E("vector", "tensor_copy", hcol[:, 0:8], ps_t[:, 0:8], inc=1)
        bank, bi = acq()
        for m in range(DC):
            wi = m % 2
            wos = [wos0, wos1][wi]
            W("sync", sV, max(wo_state["cast_last"], _ws_state["user"][1]))
            v = dma_in(AP(wosf, 1024, [[2112, 128], [128, DC], [1, 128]]),
                       AP(woT, 128 * m, [[D, 128], [128 * D, DC], [1, 128]]))
            W("vector", sLD, v)
            W("vector", sT, wo_state["user"][wi])
            wo_state["cast"][wi] = E("vector", "tensor_copy", wos[:, :],
                                     wosf[:, 1024:2048], inc=1)
            wo_state["cast_last"] = wo_state["cast"][wi]
            _ws_state["user"][1] = wo_state["cast"][wi]
            W("tensor", sV, max(v_aoh, wo_state["cast_last"]))
            for k in range(DC):
                MM(bank[:, m:m + 1], wos[:, 128 * k:128 * k + 128],
                   hcol[:, k:k + 1], (m == 0 and k == 0), k == DC - 1,
                   inc=(1 if k == DC - 1 else 0))
            wo_state["user"][wi] = sT.n
        g_wo = sT.n
        W("vector", sT, g_wo)
        v_aoh = E("vector", "tensor_copy", hcol[:, 8:16], bank[:, 0:8],
                  inc=1)
        rel(bi, v_aoh)

        def col_ln(src_bf8, sq_bf8, vv):
            E("vector", "tensor_tensor", sq_bf8, src_bf8, src_bf8, ALU.mult,
              inc=1)
            v_s = max(vv, sV.n)
            bank, bi = acq()
            W("tensor", sV, v_s)
            MM(bank[0:1, 0:8], ONES_B, src_bf8, True, True)
            g = MM(bank[0:1, 8:16], ONES_B, sq_bf8, False, True, inc=1)
            W("vector", sT, g)
            E("vector", "tensor_reduce", rowc(0, 2, 0, 1), bank[0:1, 0:8],
              AX.X, ALU.add)
            E("vector", "tensor_reduce", rowc(0, 2, 1, 2), bank[0:1, 8:16],
              AX.X, ALU.add, inc=1)
            rel(bi, sV.n)
            E("vector", "tensor_scalar", rowc(0, 2, 0, 1), rowc(0, 2, 0, 1),
              1.0 / D, None, ALU.mult)
            E("vector", "tensor_scalar", rowc(0, 2, 1, 2), rowc(0, 2, 1, 2),
              1.0 / D, None, ALU.mult)
            E("vector", "tensor_tensor", rowc(0, 2, 2, 3), rowc(0, 2, 0, 1),
              rowc(0, 2, 0, 1), ALU.mult)
            v = E("vector", "tensor_tensor", rowc(0, 2, 1, 2),
                  rowc(0, 2, 1, 2), rowc(0, 2, 2, 3), ALU.subtract, inc=1)
            W("scalar", sV, v)
            E("scalar", "activation", rowc(0, 2, 1, 2), rowc(0, 2, 1, 2),
              ACTF.Ln, bias=constf[0:1, 388:389], scale=1.0)
            c = E("scalar", "activation", rowc(0, 2, 1, 2), rowc(0, 2, 1, 2),
                  ACTF.Exp, bias=0.0, scale=-0.5, inc=1)
            bank, bi = acq()
            W("tensor", sC, c)
            MM(bank[:, 0:1], ONES_ROW_F, rowc(0, 2, 0, 1), True, True)
            g = MM(bank[:, 1:2], ONES_ROW_F, rowc(0, 2, 1, 2), False, True,
                   inc=1)
            W("vector", sT, g)
            v = E("vector", "tensor_copy", smallf[:, 8:10], bank[:, 0:2],
                  inc=1)
            rel(bi, v)
            return v

        v_ln3h = col_ln(hcol[:, 8:16], hcol[:, 24:32], v_aoh)
        W("vector", sV, v_ln3h)
        E("vector", "tensor_tensor", smallf[:, 10:11], smallf[:, 8:9],
          smallf[:, 9:10], ALU.mult)
        E("vector", "scalar_tensor_tensor", tmpf[:, 0:8], hcol[:, 8:16],
          smallf[:, 9:10],
          AP(x1, 0, [[DC * BT, 128], [BT, 8], [1, 1]]), ALU.mult, ALU.add)
        v_xmh = E("vector", "tensor_scalar", tmpf[:, 0:8], tmpf[:, 0:8],
                  smallf[:, 10:11], None, ALU.subtract, inc=1)

        hstate = {"v_carry": 0, "v_h1s": 0}
        def halo_part2():
            W("vector", sV, v_xmh)
            v_xmb = E("vector", "tensor_copy", hcol[:, 16:24], tmpf[:, 0:8],
                      inc=1)
            v_ln2h = col_ln(hcol[:, 16:24], hcol[:, 24:32], v_xmb)
            W("vector", sV, v_ln2h)
            E("vector", "tensor_scalar", tmpf[:, 8:16], tmpf[:, 0:8],
              smallf[:, 8:9], None, ALU.subtract)
            E("vector", "tensor_reduce", rcol[:, 0:1], tmpf[:, 8:16],
              AX.X, ALU.max, apply_absolute_value=True, inc=1)
            v_am = sV.n
            bank, bi = acq()
            W("tensor", sV, v_am)
            g = E("tensor", "transpose", bank[0:1, 0:128], rcol[:, 0:1], IDENT_F,
                  inc=1)
            W("vector", sT, g)
            E("vector", "tensor_reduce", rowc(0, 2, 3, 4), bank[0:1, 0:128],
              AX.X, ALU.max)
            E("vector", "tensor_tensor", rowc(0, 2, 3, 4), rowc(0, 2, 3, 4),
              rowc(0, 2, 1, 2), ALU.mult)
            E("vector", "tensor_scalar", rowc(0, 2, 3, 4), rowc(0, 2, 3, 4),
              1e-5, None, ALU.max)
            E("vector", "reciprocal", rowc(0, 2, 4, 5), rowc(0, 2, 3, 4))
            E("vector", "tensor_scalar", rowc(0, 2, 4, 5), rowc(0, 2, 4, 5),
              127.0, None, ALU.mult)
            E("vector", "tensor_tensor", rowc(0, 2, 5, 6), rowc(0, 2, 1, 2),
              rowc(0, 2, 4, 5), ALU.mult)
            E("vector", "tensor_tensor", rowc(0, 2, 6, 7), rowc(0, 2, 0, 1),
              rowc(0, 2, 5, 6), ALU.mult)
            E("vector", "tensor_scalar", rowc(0, 2, 7, 8), rowc(0, 2, 3, 4),
              smallf[0:1, 4:5], 1.0 / 127.0, ALU.mult, ALU.mult, inc=1)
            v_rows_h = sV.n
            rel(bi, v_rows_h)
            bank, bi = acq()
            W("tensor", sV, v_rows_h)
            MM(bank[:, 0:1], ONES_ROW_F, rowc(0, 2, 5, 6), True, True)
            MM(bank[:, 1:2], ONES_ROW_F, rowc(0, 2, 6, 7), False, True)
            g = MM(bank[:, 2:3], ONES_ROW_F, rowc(0, 2, 7, 8), False, True, inc=1)
            W("vector", sT, g)
            v = E("vector", "tensor_copy", smallf[:, 11:14], bank[:, 0:3], inc=1)
            rel(bi, v)
            W("vector", sV, v)
            E("vector", "tensor_scalar", i32s[:, 0:8], tmpf[:, 0:8],
              smallf[:, 11:12], smallf[:, 12:13], ALU.mult, ALU.subtract)
            v_q1h = E("vector", "tensor_copy", qc[:, 0:8], i32s[:, 0:8], inc=1)

            bank, bi = acq()
            W("tensor", sV, max(v_q1h, wprep_state["v"]))
            for m in range(HC):
                for k in range(DC):
                    MM(bank[:, m:m + 1],
                       w1q[:, HN * k + 128 * m:HN * k + 128 * m + 128],
                       qc[:, k:k + 1], (m == 0 and k == 0), k == DC - 1,
                       inc=(1 if (m == HC - 1 and k == DC - 1) else 0))
            g = sT.n
            W("vector", sT, g)
            E("vector", "tensor_scalar", h1L[:, 0:HC], bank[:, 0:HC],
              smallf[:, 13:14], None, ALU.mult)
            E("vector", "tensor_scalar", h1R[:, 0:HC], h1L[:, 0:HC],
              flagf[:, 1:2], None, ALU.mult)
            hstate["v_h1s"] = E("vector", "tensor_scalar",
                                h1L[:, 0:HC], h1L[:, 0:HC],
                                flagf[:, 0:1], None, ALU.mult, inc=1)
            rel(bi, hstate["v_h1s"])
            tap("h1L", h1L[:, :], HC)
            tap("h1R", h1R[:, :], HC)
            E("vector", "memset", carry_h1[:, :], 0.0)
            hstate["v_carry"] = E("vector", "tensor_copy",
                        AP(carry_h1, 1, [[2 * HC, 128], [2, HC], [1, 1]]),
                        AP(h1L, 0, [[HC, 128], [1, HC], [1, 1]]), inc=1)


        def make_tail(h, xb, v_h1ready):
            ws_g, TW = _win(h)
            inb = 1 if h == 0 else 0
            halves = [(0, min(128, TW))]
            if TW > 128:
                halves.append((128, TW - 128))
            v_x2 = 0
            for (hoff, HW) in halves:
                W("vector", sC, tail_state["c_si"])
                for c in range(HC):
                    b0 = H1W * c + inb + hoff
                    E("vector", "tensor_scalar", yb[:, YW * c:YW * c + HW],
                      h1b[:, b0:b0 + HW],
                      cwbf[:, 3 * c:3 * c + 1], cwbf[:, 96 + c:97 + c],
                      ALU.mult, ALU.add)
                    E("vector", "scalar_tensor_tensor",
                      yb[:, YW * c:YW * c + HW], h1b[:, b0 + 1:b0 + 1 + HW],
                      cwbf[:, 3 * c + 1:3 * c + 2], yb[:, YW * c:YW * c + HW],
                      ALU.mult, ALU.add)
                    E("vector", "scalar_tensor_tensor",
                      yb[:, YW * c:YW * c + HW], h1b[:, b0 + 2:b0 + 2 + HW],
                      cwbf[:, 3 * c + 2:3 * c + 3], yb[:, YW * c:YW * c + HW],
                      ALU.mult, ALU.add, inc=(1 if c % 4 == 3 else 0))
                    if c % 4 == 3:
                        yield
                v_conv = sV.n
                tail_state["g_conv"] = 0
                quarters = [(0, min(64, HW))]
                if HW > 64:
                    quarters.append((64, HW - 64))
                for (qoff, QW) in quarters:
                    if h == 0 and hoff == 0 and qoff == 0:
                        tap("yb_q0", yb[:, :], HC * YW)
                    W("scalar", sV, max(v_conv, tail_state["v_h2q"]))
                    c_si = E("scalar", "activation",
                             AP(h2f, 0, [[HC * 66, 128], [66, HC], [1, QW]]),
                             AP(yb, qoff, [[HC * YW, 128], [YW, HC], [1, QW]]),
                             ACTF.Silu, inc=1)
                    tail_state["c_si"] = c_si
                    W("vector", sC, c_si)
                    E("vector", "tensor_reduce", amax1[:, 0:QW],
                      AP(h2f, 0, [[HC * 66, 128], [1, QW], [66, HC]]),
                      AX.X, ALU.max, apply_absolute_value=True, inc=1)
                    v_amq = sV.n
                    bank, bi = acq()
                    W("tensor", sV, v_amq)
                    g = E("tensor", "transpose", bank[0:QW, 0:128],
                          amax1[:, 0:QW], IDENT_F, inc=1)
                    W("vector", sT, g)
                    E("vector", "tensor_reduce", rcol[0:QW, 4:5],
                      bank[0:QW, 0:128], AX.X, ALU.max, inc=1)
                    v_s2c = sV.n
                    rel(bi, v_s2c)
                    bank, bi = acq()
                    W("tensor", sV, v_s2c)
                    g = E("tensor", "transpose", bank[0:1, 0:QW],
                          rcol[0:QW, 4:5], IDENT_F[0:QW, 0:QW], inc=1)
                    W("vector", sT, g)
                    E("vector", "tensor_scalar", rowc(1, 1, 0, QW),
                      bank[0:1, 0:QW], 1e-5, None, ALU.max)
                    E("vector", "tensor_scalar", rowc(2, 1, 0, QW),
                      rowc(1, 1, 0, QW), smallf[0:1, 6:7], 1.0 / 127.0,
                      ALU.mult, ALU.mult)
                    E("vector", "reciprocal", rowc(1, 1, 0, QW),
                      rowc(1, 1, 0, QW))
                    E("vector", "tensor_scalar", rowc(1, 1, 0, QW),
                      rowc(1, 1, 0, QW), 127.0, None, ALU.mult, inc=1)
                    v = sV.n
                    rel(bi, v)
                    vs2 = replicate(rowc(1, 1, 0, QW), 0, QW, after_v=v)
                    va2 = replicate(rowc(2, 1, 0, QW), 128, QW, after_v=v)
                    W("vector", sV, max(vs2, va2))
                    yield
                    for g8 in range(8):
                        E("vector", "tensor_tensor",
                          AP(i32s, 0, [[264, 128], [QW, 4], [1, QW]]),
                          AP(h2f, 66 * 4 * g8,
                             [[HC * 66, 128], [66, 4], [1, QW]]),
                          rep_b(0, 4, QW), ALU.mult)
                        E("vector", "tensor_copy",
                          AP(h2qb, 66 * 4 * g8,
                             [[HC * 66, 128], [66, 4], [1, QW]]),
                          AP(i32s, 0, [[264, 128], [QW, 4], [1, QW]]),
                          inc=(1 if g8 % 4 == 3 else 0))
                        if g8 == 3:
                            yield
                    v_h2q = sV.n
                    tail_state["v_h2q"] = v_h2q
                    yield
                    if h == 0 and hoff == 0 and qoff == 0:
                        tap("h2f_q0", h2f[:, :], HC * 66, dt="f")
                        tap("h2q_q0", h2qb[:, :], HC * 66)
                        tap("reps_q0", reps[:, 0:256], 256, dt="f")
                    for m0 in range(0, DC, 4):
                        bank, bi = acq()
                        W("tensor", sV, max(v_h2q, wprep_state["v"]))
                        for j in range(4):
                            m = m0 + j
                            for k in range(HC):
                                MM(bank[:, 128 * j:128 * j + QW],
                                   w2q[:, D * k + 128 * m:
                                       D * k + 128 * m + 128],
                                   h2qb[:, 66 * k:66 * k + QW],
                                   (j == 0 and k == 0), k == HC - 1,
                                   inc=(1 if (j == 3 and k == HC - 1) else 0))
                            if j == 1:
                                yield
                        g = sT.n
                        W("vector", sT, g)
                        v_x2 = E("vector", "tensor_tensor",
                                 AP(x2t, X2W * m0 + hoff + qoff,
                                    [[DC * X2W, 128], [X2W, 4], [1, QW]]),
                                 AP(bank, 0, [[512, 128], [128, 4], [1, QW]]),
                                 rep_b(128, 4, QW), ALU.mult, inc=1)
                        rel(bi, v_x2)
                        yield
            tap(f"x2_{h}", x2t[:, :], DC * X2W)
            W("vector", sV, v_x2)
            E("vector", "tensor_tensor",
              AP(sqb, 0, [[DC * 260, 128], [260, DC], [1, TW]]),
              AP(x2, 0, [[DC * X2W, 128], [X2W, DC], [1, TW]]),
              AP(x2, 0, [[DC * X2W, 128], [X2W, DC], [1, TW]]),
              ALU.mult, inc=1)
            v_sq4 = sV.n
            yield
            bankA, biA = acq()
            bankB, biB = acq()
            W("tensor", sV, v_sq4)
            for f in range(DC):
                MM(bankA[0:1, 0:TW], ONES_B, x2[:, f * X2W:f * X2W + TW],
                   f == 0, f == DC - 1)
            for f in range(DC):
                MM(bankB[0:1, 0:TW], ONES_B, sqb[:, f * 260:f * 260 + TW],
                   f == 0, f == DC - 1, inc=(1 if f == DC - 1 else 0))
            g = sT.n
            W("vector", sT, g)
            E("vector", "tensor_scalar", rowc(0, 0, 0, TW), bankA[0:1, 0:TW],
              1.0 / D, None, ALU.mult)
            E("vector", "tensor_scalar", rowc(3, 0, 0, TW), bankB[0:1, 0:TW],
              1.0 / D, None, ALU.mult)
            E("vector", "tensor_tensor", rowc(1, 0, 0, TW), rowc(0, 0, 0, TW),
              rowc(0, 0, 0, TW), ALU.mult)
            v = E("vector", "tensor_tensor", rowc(3, 0, 0, TW),
                  rowc(3, 0, 0, TW), rowc(1, 0, 0, TW), ALU.subtract, inc=1)
            rel(biA, v)
            rel(biB, v)
            W("scalar", sV, v)
            E("scalar", "activation", rowc(3, 0, 0, TW), rowc(3, 0, 0, TW),
              ACTF.Ln, bias=constf[0:1, 388:389], scale=1.0)
            c4 = E("scalar", "activation", rowc(1, 0, 0, TW),
                   rowc(3, 0, 0, TW), ACTF.Exp, bias=0.0, scale=-0.5, inc=1)
            W("vector", sC, c4)
            E("vector", "tensor_tensor", rowc(2, 0, 0, TW), rowc(0, 0, 0, TW),
              rowc(1, 0, 0, TW), ALU.mult, inc=1)
            v_ln4 = sV.n
            v1 = replicate(rowc(1, 0, 0, TW), 0, TW, after_v=v_ln4)
            v2 = replicate(rowc(2, 0, 0, TW), 320, TW, after_v=v_ln4)
            W("vector", sV, max(v1, v2))
            yield
            if h == 0:
                E("vector", "tensor_tensor",
                  AP(sqb, 0, [[DC * 260, 128], [260, DC], [1, TW]]),
                  AP(x2, 0, [[DC * X2W, 128], [X2W, DC], [1, TW]]),
                  rep_b(0, DC, TW), ALU.mult)
                E("vector", "tensor_tensor",
                  AP(sqb, 0, [[DC * 260, 128], [260, DC], [1, TW]]),
                  AP(sqb, 0, [[DC * 260, 128], [260, DC], [1, TW]]),
                  rep_b(320, DC, TW), ALU.subtract, inc=1)
                yield
                v_out = E("vector", "tensor_tensor",
                          AP(xb, 0, [[DC * BT, 128], [BT, DC], [1, TW]]),
                          AP(xb, 0, [[DC * BT, 128], [BT, DC], [1, TW]]),
                          AP(sqb, 0, [[DC * 260, 128], [260, DC], [1, TW]]),
                          ALU.add, inc=1)
            else:
                E("vector", "tensor_tensor", tmpf[:, 0:DC],
                  AP(x2, 0, [[DC * X2W, 128], [X2W, DC], [1, 1]]),
                  rep_b(0, DC, 1), ALU.mult)
                E("vector", "tensor_tensor", tmpf[:, 0:DC], tmpf[:, 0:DC],
                  rep_b(320, DC, 1), ALU.subtract)
                E("vector", "tensor_tensor",
                  tails_s[:, DC * (h - 1):DC * h], tmpf[:, 0:DC],
                  carry_xm[:, :], ALU.add, inc=1)
                E("vector", "tensor_tensor",
                  AP(sqb, 0, [[DC * 260, 128], [260, DC], [1, TW - 1]]),
                  AP(x2, 1, [[DC * X2W, 128], [X2W, DC], [1, TW - 1]]),
                  rep_b(0, DC, TW - 1, off=1), ALU.mult)
                E("vector", "tensor_tensor",
                  AP(sqb, 0, [[DC * 260, 128], [260, DC], [1, TW - 1]]),
                  AP(sqb, 0, [[DC * 260, 128], [260, DC], [1, TW - 1]]),
                  rep_b(320, DC, TW - 1, off=1), ALU.subtract, inc=1)
                yield
                v_out = E("vector", "tensor_tensor",
                          AP(xb, 0, [[DC * BT, 128], [BT, DC], [1, TW - 1]]),
                          AP(xb, 0, [[DC * BT, 128], [BT, DC], [1, TW - 1]]),
                          AP(sqb, 0, [[DC * 260, 128], [260, DC], [1, TW - 1]]),
                          ALU.add, inc=1)
            if h < NB - 1:
                E("vector", "tensor_copy", carry_xm[:, :],
                  AP(xb, 255, [[DC * BT, 128], [BT, DC], [1, 1]]), inc=1)
            wst = 255 if h < NB - 1 else 256
            W("sync", sV, v_out)
            v_store[h] = dma_out(
                AP(outT, 256 * h, [[T, 128], [128 * T, DC], [1, wst]]),
                AP(xb, 0, [[DC * BT, 128], [BT, DC], [1, wst]]))
            if h + 2 < NB:
                W("sync", sST, v_store[h])
                v_xload[h + 2] = dma_in(
                    AP(xbufs[(h + 2) % 2], 0,
                       [[DC * BT, 128], [BT, DC], [1, BT]]),
                    AP(xT, BT * (h + 2), [[T, 128], [128 * T, DC], [1, BT]]))
            yield

        xbufs = [x0, x1]
        v_xload = [0] * NB
        v_store = [0] * NB
        v_xload[0] = dma_in(AP(x0, 0, [[DC * BT, 128], [BT, DC], [1, BT]]),
                            AP(xT, 0, [[T, 128], [128 * T, DC], [1, BT]]))
        W("sync", sV, v_xmh)
        v_xload[1] = dma_in(AP(x1, 0, [[DC * BT, 128], [BT, DC], [1, BT]]),
                            AP(xT, BT, [[T, 128], [128 * T, DC], [1, BT]]))

        c_exp_last = c_he
        x2 = x2t
        tail_state = {"c_si": 0, "v_h2q": 0, "g_conv": 0}
        if _EAGER_WPREP:
            for _ in wprep_gen():
                pass
            tail_gen = None
            halo_part2()
        else:
            tail_gen = wprep_gen()
        def emit_head(h):
            xb = xbufs[h % 2]
            v_xn1 = phase_ln1_qkv(xb, v_xload[h])
            tap(f"xn_{h}", xn[:, :], DC * BT)
            v_q = proj(wqb, xn, xq, v_xn1, v_wcast)
            v_k = proj(wkb, xn, xk, v_xn1, v_wcast)
            v_v = proj_v(xn, v_xn1, v_wcast)
            tap(f"xq_{h}", xq[:, :], DC * BT)
            tap(f"xk_{h}", xk[:, :], DC * BT)
            heat(12)
            v_bias = phase_bound(v_q, v_k, c_exp_last)
            tap(f"bias_{h}", smallf[:, 0:1], 1, dt="f")
            return (v_q, v_k, v_v, v_bias)

        head = emit_head(0)
        for h in range(NB):
            xb = xbufs[h % 2]
            (v_q, v_k, v_v, v_bias) = head
            c_exp_last, v_ao = phase_attention(v_q, v_k, v_v, v_bias, xn,
                                               inject=tail_gen)
            if tail_gen is not None:
                for _ in tail_gen:
                    pass
                tail_gen = None
            tap(f"ao_{h}", xn[:, :], DC * BT)
            v_ao2 = proj_wo(v_ao)
            heat(20)
            tap(f"ao2_{h}", ptb[:, 0:DC * BT], DC * BT)
            v_ln3 = ln_stats(ptb, BT, BT)
            v1 = replicate(rowc(1, 0, 0, BT), 0, BT, after_v=v_ln3)
            v2 = replicate(rowc(2, 0, 0, BT), 256, BT, after_v=v_ln3)
            W("vector", sV, max(v1, v2, v_ao2))
            E("vector", "tensor_tensor",
              AP(sqb, 0, [[DC * 260, 128], [260, DC], [1, BT]]),
              AP(ptb, 0, [[2064, 128], [BT, DC], [1, BT]]),
              rep_b(0, DC, BT), ALU.mult)
            E("vector", "tensor_tensor",
              AP(sqb, 0, [[DC * 260, 128], [260, DC], [1, BT]]),
              AP(sqb, 0, [[DC * 260, 128], [260, DC], [1, BT]]),
              rep_b(256, DC, BT), ALU.subtract)
            v_xmid = E("vector", "tensor_tensor",
                       AP(xb, 0, [[DC * BT, 128], [BT, DC], [1, BT]]),
                       AP(xb, 0, [[DC * BT, 128], [BT, DC], [1, BT]]),
                       AP(sqb, 0, [[DC * 260, 128], [260, DC], [1, BT]]),
                       ALU.add, inc=1)
            tap(f"xmid_{h}", xb[:, :], DC * BT, dt="f")
            if h + 1 < NB:
                head = emit_head(h + 1)
            if h == 0 and not _EAGER_WPREP:
                halo_part2()
            heat(12)
            W("vector", sV, v_xmid)
            E("vector", "tensor_copy", xn[:, :], xb[:, :], inc=1)
            v_ln2 = ln_stats(xn, BT, BT)
            v_m2 = replicate(rowc(0, 0, 0, BT), 0, BT, after_v=v_ln2)
            W("vector", sV, v_m2)
            W("vector", sC, c_exp_last)
            E("vector", "tensor_tensor",
              AP(ps_s, 0, [[2048, 128], [256, DC], [1, BT]]),
              AP(xb, 0, [[DC * BT, 128], [BT, DC], [1, BT]]),
              rep_b(0, DC, BT), ALU.subtract, inc=1)
            E("vector", "tensor_reduce", amax1[:, 0:BT],
              AP(ps_s, 0, [[2048, 128], [1, BT], [256, DC]]),
              AX.X, ALU.max, apply_absolute_value=True, inc=1)
            v_am1 = sV.n
            for u in range(2):
                bank, bi = acq()
                W("tensor", sV, v_am1 if u == 0 else sV.n)
                g = E("tensor", "transpose", bank[:, 0:128],
                      amax1[:, 128 * u:128 * u + 128], IDENT_F, inc=1)
                W("vector", sT, g)
                E("vector", "tensor_reduce", rcol[:, 4:5], bank[:, 0:128],
                  AX.X, ALU.max, inc=1)
                v = sV.n
                rel(bi, v)
                bank, bi = acq()
                W("tensor", sV, v)
                g = E("tensor", "transpose", bank[0:1, 0:128], rcol[:, 4:5],
                      IDENT_F, inc=1)
                W("vector", sT, g)
                E("vector", "tensor_copy", rowc(0, 3, 128 * u, 128 * u + 128),
                  bank[0:1, 0:128], inc=1)
                rel(bi, sV.n)
            E("vector", "tensor_tensor", rowc(0, 3, 0, BT), rowc(0, 3, 0, BT),
              rowc(1, 0, 0, BT), ALU.mult)
            E("vector", "tensor_scalar", rowc(0, 3, 0, BT), rowc(0, 3, 0, BT),
              1e-5, None, ALU.max)
            E("vector", "reciprocal", rowc(1, 3, 0, BT), rowc(0, 3, 0, BT))
            E("vector", "tensor_scalar", rowc(1, 3, 0, BT), rowc(1, 3, 0, BT),
              127.0, None, ALU.mult)
            E("vector", "tensor_tensor", rowc(2, 3, 0, BT), rowc(1, 0, 0, BT),
              rowc(1, 3, 0, BT), ALU.mult)
            E("vector", "tensor_scalar", rowc(1, 3, 0, BT), rowc(0, 3, 0, BT),
              smallf[0:1, 4:5], 1.0 / 127.0, ALU.mult, ALU.mult, inc=1)
            v_chain = sV.n
            va = replicate(rowc(2, 3, 0, BT), 256, BT, after_v=v_chain)
            vc = replicate(rowc(1, 3, 0, BT), 640, BT, after_v=v_chain)
            W("vector", sV, max(va, vc))
            for f in range(DC):
                E("vector", "tensor_tensor", i32s[:, 0:BT],
                  ps_s[:, f * BT:f * BT + BT], reps[:, 256:256 + BT],
                  ALU.mult)
                E("vector", "tensor_copy", sqb[:, f * 260:f * 260 + BT],
                  i32s[:, 0:BT], inc=(1 if f == DC - 1 else 0))
            v_x1q = sV.n
            tap(f"x1q_{h}", sqb[:, :], DC * 260)
            W("vector", sV, hstate["v_carry"])
            E("vector", "tensor_copy",
              AP(h1b, 0, [[HC * H1W, 128], [H1W, HC], [1, 2]]),
              AP(carry_h1, 0, [[2 * HC, 128], [2, HC], [1, 2]]), inc=1)
            if h == NB - 1:
                W("vector", sV, hstate["v_h1s"])
                E("vector", "tensor_copy",
                  AP(h1b, 258, [[HC * H1W, 128], [H1W, HC], [1, 1]]),
                  AP(h1R, 0, [[HC, 128], [1, HC], [1, 1]]), inc=1)
            heat(10)
            for m0 in range(0, HC, 2):
                bank, bi = acq()
                W("tensor", sV, max(v_x1q, wprep_state["v"]))
                for j in range(2):
                    m = m0 + j
                    for k in range(DC):
                        MM(bank[:, 256 * j:256 * j + BT],
                           w1q[:, HN * k + 128 * m:HN * k + 128 * m + 128],
                           sqb[:, k * 260:k * 260 + BT],
                           (j == 0 and k == 0), k == DC - 1,
                           inc=(1 if (j == 1 and k == DC - 1) else 0))
                g = sT.n
                W("vector", sT, g)
                v = E("vector", "tensor_tensor",
                      AP(h1b, H1W * m0 + 2,
                         [[HC * H1W, 128], [H1W, 2], [1, BT]]),
                      AP(bank, 0, [[512, 128], [256, 2], [1, BT]]),
                      rep_b(640, 2, BT), ALU.mult, inc=1)
                rel(bi, v)
            E("vector", "tensor_copy",
              AP(carry_h1, 0, [[2 * HC, 128], [2, HC], [1, 2]]),
              AP(h1b, 256, [[HC * H1W, 128], [H1W, HC], [1, 2]]), inc=1)
            hstate["v_carry"] = sV.n
            v_h1ready = sV.n
            tap(f"repsF_{h}", reps[:, :], 896, dt="f")
            tap(f"smallf_{h}", smallf[:, :], 16, dt="f")
            tap(f"h1_{h}", h1b[:, :], HC * H1W)
            tail_gen = make_tail(h, xb, v_h1ready)

        if tail_gen is not None:
            for _ in tail_gen:
                pass
            tail_gen = None


        dma_out(tails_t.ap(), tails_s[:, :])
        Q["sync"].append(lambda e, v=sST.n: e.wait_ge(sST.h, v))

        @blk.sync
        def _(e):
            for f in Q["sync"]:
                f(e)

        @blk.tensor
        def _(e):
            for f in Q["tensor"]:
                f(e)

        @blk.vector
        def _(e):
            for f in Q["vector"]:
                f(e)

        @blk.scalar
        def _(e):
            for f in Q["scalar"]:
                f(e)

        @blk.gpsimd
        def _(e):
            for f in Q["gpsimd"]:
                f(e)

    _es.close()
    return nc


_NC_CACHE = {}


def _get_nc():
    if "nc" not in _NC_CACHE:
        _NC_CACHE["nc"] = build_program()
    return _NC_CACHE["nc"]


def _consts_np():
    c = np.zeros((128, 392), np.float32)
    c[:, 0:128] = np.eye(128, dtype=np.float32)
    c[:, 128] = 1.0
    c[0:64, 129] = 1.0
    c[64:128, 130] = 1.0
    sw = np.zeros((128, 128), np.float32)
    for m in range(128):
        sw[(m + 64) % 128, m] = 1.0
    c[:, 131:259] = sw
    for b in (0, 32, 64, 96):
        c[b, 260:388] = 1.0
    c[:, 388] = 1e-5
    c[:, 389] = 1e-30
    return c


def kernel(L0_attn_x, L0_attn_wq, L0_attn_wk, L0_attn_wv, L0_attn_wo,
           L0_ffn_w1, L0_ffn_conv_w, L0_ffn_conv_b, L0_ffn_w2):
    x = np.ascontiguousarray(np.asarray(L0_attn_x, dtype=np.float32))
    wq = np.asarray(L0_attn_wq, np.float32)
    wk = np.asarray(L0_attn_wk, np.float32)
    wv = np.asarray(L0_attn_wv, np.float32)
    wo = np.asarray(L0_attn_wo, np.float32)
    w1 = np.asarray(L0_ffn_w1, np.float32)
    cw = np.asarray(L0_ffn_conv_w, np.float32)
    cb = np.asarray(L0_ffn_conv_b, np.float32)
    w2 = np.asarray(L0_ffn_w2, np.float32)

    wqT = np.ascontiguousarray(wq.T)
    wkT = np.ascontiguousarray(wk.T)
    wvT = np.ascontiguousarray(wv.T)
    woT = np.ascontiguousarray(wo.T)
    w1T = np.ascontiguousarray(w1.T)
    w2T = np.ascontiguousarray(w2.T)

    consts = _consts_np()
    cwb = np.zeros((128, 128), np.float32)
    cwr = cw.reshape(HC, 128, 3)
    for c in range(HC):
        cwb[:, 3 * c:3 * c + 3] = cwr[c]
    cbr = cb.reshape(HC, 128)
    for c in range(HC):
        cwb[:, 96 + c] = cbr[c]

    in_maps = make_in_maps(x, wqT, wkT, wvT, woT, w1T, w2T, consts, cwb)
    nc = _get_nc()
    res = run_bass_kernel_spmd(nc, in_maps, core_ids=list(range(8)))
    return _assemble(res)


def make_in_maps(x, wqT, wkT, wvT, woT, w1T, w2T, consts, cwb):
    in_maps = []
    for core in range(8):
        b, half = core // 2, core % 2
        xs = np.ascontiguousarray(x[b, T * half:T * half + T].T)
        if half == 0:
            hb = x[b, T:T + BT]
            fl = (0.0, 1.0)
        else:
            hb = np.roll(x[b, T - BT:T], 1, axis=0)
            fl = (1.0, 0.0)
        xTh = np.ascontiguousarray(hb.T)
        flags = np.zeros((128, 2), np.float32)
        flags[:, 0] = fl[0]
        flags[:, 1] = fl[1]
        in_maps.append({
            "xT": xs, "xTh": xTh, "flags": flags,
            "wqT": wqT, "wkT": wkT, "wvT": wvT, "woT": woT,
            "w1T": w1T, "w2T": w2T, "consts": consts, "cwb": cwb,
        })
    return in_maps


def _assemble(res):
    out = np.empty((4, 4096, D), np.float32)
    for core in range(8):
        b, half = core // 2, core % 2
        o = np.asarray(res.results[core]["outT"])
        tails = np.asarray(res.results[core]["out_tails"])
        ob = o.T.copy()
        for h in range(NB - 1):
            ob[256 * h + 255, :] = tails[:, DC * h:DC * (h + 1)].T.reshape(-1)
        out[b, T * half:T * half + T] = ob
    return out
